# version 19
# baseline (speedup 1.0000x reference)
import os
import numpy as np
from contextlib import ExitStack

_PHASE_CAP = int(os.environ.get("KPHASES", "99"))

import concourse.bass as bass
import concourse.mybir as mybir
from concourse.bass_utils import run_bass_kernel_spmd

F32 = mybir.dt.float32
BF16 = mybir.dt.bfloat16
U32 = mybir.dt.uint32
AF = mybir.ActivationFunctionType
ALU = mybir.AluOpType
AX = mybir.AxisListType

B, S, E = 4, 2048, 2048
H, D = 16, 128
CH = 256
NCH = 4
T = 1024
NTT = 8
KC = 16
HGU = 16384
HP = 8192
NJJ = 64
LN_EPS = 1e-5
RMS_EPS = float(np.finfo(np.float32).eps)

ENGS = ("sync", "gpsimd", "tensor", "vector", "scalar")


class Sched:

    def __init__(self, nc, stack):
        self.nc = nc
        self.stack = stack
        self.q = {en: [] for en in ENGS}
        self.sems = {}
        for name in ("g16", "s16", "a16", "pe", "dv", "sc", "bar"):
            self._sem(name)

    def _sem(self, name):
        if name not in self.sems:
            self.sems[name] = [self.stack.enter_context(self.nc.semaphore(name)), 0]
        return self.sems[name]

    def n(self, sem):
        return self._sem(sem)[1]

    def emit(self, eng, fn, inc=None, k=1):
        if inc is None:
            self.q[eng].append(fn)
            return None
        h = self._sem(inc)[0]
        self._sem(inc)[1] += k

        def wrapped(e, fn=fn, h=h, k=k):
            fn(e).then_inc(h, k)

        self.q[eng].append(wrapped)
        return self.sems[inc][1]

    def inc(self, eng, sem, k=1):
        h = self._sem(sem)[0]
        self._sem(sem)[1] += k
        self.q[eng].append(lambda e, h=h, k=k: e.sem_inc(h, k))
        return self.sems[sem][1]

    def wait(self, eng, sem, val):
        if val is None or val <= 0:
            return
        h = self._sem(sem)[0]
        self.q[eng].append(lambda e, h=h, v=val: e.wait_ge(h, v))

    def barrier(self):
        h = self.sems["bar"][0]
        self.sems["bar"][1] += 5
        tgt = self.sems["bar"][1]
        for en in ENGS:
            def f(e, h=h, tgt=tgt):
                e.sem_inc(h, 1)
                e.wait_ge(h, tgt)
            self.q[en].append(f)

    def flush(self, block):
        for en in ENGS:
            fns = self.q[en]
            if not fns:
                continue

            def run(e, fns=fns):
                for fn in fns:
                    fn(e)

            getattr(block, en)(run)
        self.q = {en: [] for en in ENGS}


def build_nc():
    nc = bass.Bass("TRN2", target_bir_lowering=False)

    def din(name, shape, dtype=F32):
        return nc.dram_tensor(name, list(shape), dtype, kind="ExternalInput").ap()

    io = {}
    io["x_T"] = din("x_T", [E, T])
    io["x_tok"] = din("x_tok", [T, E])
    io["xaux_T"] = din("xaux_T", [E, T])
    io["wqkv_T"] = din("wqkv_T", [E, 3 * E])
    io["wout_T"] = din("wout_T", [E, E])
    io["wgu_T"] = din("wgu_T", [E, HGU])
    io["wp_T"] = din("wp_T", [HP, E])
    io["cos_m"] = din("cos_m", [T, D])
    io["sinA_m"] = din("sinA_m", [T, D])
    io["cos_a"] = din("cos_a", [T, D])
    io["sinA_a"] = din("sinA_a", [T, D])
    io["auxones"] = din("auxones", [T, 1])
    io["trilm"] = din("trilm", [128, 2 * CH])
    io["ident"] = din("ident", [128, 128])
    io["ind2"] = din("ind2", [2, 128])
    io["grp2"] = din("grp2", [128, 2])
    io["mgrp"] = din("mgrp", [128, 128])
    io["onescol"] = din("onescol", [128, 1])
    io["out"] = nc.dram_tensor("out", [T, E], F32, kind="ExternalOutput").ap()

    io["wqgu_dram"] = nc.dram_tensor("wqgu_scr", [NJJ, KC, 128, 256], BF16, kind="Internal").ap()
    io["wqp_dram"] = nc.dram_tensor("wqp_scr", [64, 128, E], BF16, kind="Internal").ap()
    io["y1_dram"] = nc.dram_tensor("y1_scr", [T, E], F32, kind="Internal").ap()
    io["hsil_dram"] = nc.dram_tensor("hsil_scr", [64, 128, T], BF16, kind="Internal").ap()
    io["outpre_dram"] = nc.dram_tensor("outpre_scr", [T, E], F32, kind="Internal").ap()
    io["inv_dram"] = nc.dram_tensor("inv_scr", [1, T], F32, kind="Internal").ap()
    if os.environ.get("KDBG") in ("wq", "wq2"):
        io["dbg_wqgu"] = nc.dram_tensor("dbg_wqgu", [NJJ, KC, 128, 256], BF16, kind="ExternalOutput").ap()
        io["dbg_wqp"] = nc.dram_tensor("dbg_wqp", [64, 128, E], BF16, kind="ExternalOutput").ap()
    if os.environ.get("KDBG") == "qkv":
        io["dbg_qT"] = nc.dram_tensor("dbg_qT", [128, H, T], BF16, kind="ExternalOutput").ap()
        io["dbg_kT"] = nc.dram_tensor("dbg_kT", [128, H, T], BF16, kind="ExternalOutput").ap()
        io["dbg_v"] = nc.dram_tensor("dbg_v", [128, NTT, H, 129], BF16, kind="ExternalOutput").ap()
        io["dbg_sinit"] = nc.dram_tensor("dbg_sinit", [128, H, 129], F32, kind="ExternalOutput").ap()
    if os.environ.get("KDBG") == "numt":
        io["dbg_numT"] = nc.dram_tensor("dbg_numT", [128, H, T], BF16, kind="ExternalOutput").ap()
    if os.environ.get("KDBG") == "y1":
        io["dbg_y1"] = nc.dram_tensor("dbg_y1", [T, E], F32, kind="ExternalOutput").ap()
    if os.environ.get("KDBG") == "p4":
        io["dbg_hsil"] = nc.dram_tensor("dbg_hsil", [64, 128, T], BF16, kind="ExternalOutput").ap()
        io["dbg_inv"] = nc.dram_tensor("dbg_inv", [128, NTT], F32, kind="ExternalOutput").ap()
    if os.environ.get("KDBG") == "outpre":
        io["dbg_outpre"] = nc.dram_tensor("dbg_outpre", [T, E], F32, kind="ExternalOutput").ap()

    st = ExitStack()
    with st:
        sch = Sched(nc, st)
        block = st.enter_context(nc.Block())

        sb = {}

        def alloc(stack, name, shape, dtype):
            t = stack.enter_context(nc.sbuf_tensor(name, list(shape), dtype))
            sb[name] = t
            return t

        ident_bf = alloc(st, "ident_bf", [128, 128], BF16)
        ind2_bf = alloc(st, "ind2_bf", [2, 128], BF16)
        grp2_bf = alloc(st, "grp2_bf", [128, 2], BF16)
        mgrp_bf = alloc(st, "mgrp_bf", [128, 128], BF16)
        onescol_bf = alloc(st, "onescol_bf", [128, 1], BF16)
        trilm_bf = alloc(st, "trilm_bf", [128, 2 * CH], BF16)
        inv_sb = alloc(st, "inv_sb", [128, NTT], F32)
        two_col = alloc(st, "two_col", [128, 1], F32)
        sch.emit("vector", lambda v: v.memset(two_col[:], 2.0))

        for dname, iname in (("ident_bf", "ident"), ("ind2_bf", "ind2"), ("grp2_bf", "grp2"),
                             ("mgrp_bf", "mgrp"), ("onescol_bf", "onescol"), ("trilm_bf", "trilm")):
            sch.emit("gpsimd",
                     lambda g, d=sb[dname], s=io[iname]: g.dma_start(out=d[:], in_=s),
                     inc="g16", k=16)
        sch.wait("gpsimd", "g16", sch.n("g16"))
        sch.barrier()

        st_q = ExitStack()
        quant_alloc(nc, st_q, sb)
        st_num = ExitStack()
        st_attn = ExitStack()
        st_y1 = ExitStack()
        alloc(st_attn, "qT", [128, H, T], BF16)
        alloc(st_attn, "kT", [128, H, T], BF16)
        alloc(st_attn, "v_sb", [128, NTT, H, 129], BF16)
        alloc(st_attn, "S_init", [128, H, 129], F32)

        qgen = quant_units(nc, sch, io, sb)
        qstate = {"pulled": 0, "units": 0.0}
        NQT = (256 + 128) * 3
        W1A, W1B, W2, W3, W4 = 1.0, 1.0, 0.6, 1.0, 1.45
        NUNITS_W = 96 * W1A + 64 * W1B + 64 * W2 + 8 * W3 + 128 * W4

        def tick(w=1.0):
            qstate["units"] += w
            want = min(NQT, int(round(qstate["units"] * (NQT / NUNITS_W))))
            while qstate["pulled"] < want:
                next(qgen)
                qstate["pulled"] += 1

        def drain(upto):
            while qstate["pulled"] < upto:
                next(qgen)
                qstate["pulled"] += 1

        phase_p1a(nc, sch, io, sb, lambda: tick(W1A))
        sch.barrier()
        phase_p1b(nc, sch, io, sb, lambda: tick(W1B))
        sch.barrier()
        sb["numT"] = st_num.enter_context(
            nc.sbuf_tensor("numT", [128, H, T], BF16, side="right"))
        phase_p2(nc, sch, io, sb, lambda: tick(W2))
        sch.barrier()
        st_attn.close()
        sb["y1T"] = st_y1.enter_context(nc.sbuf_tensor("y1T", [128, KC, T], BF16))
        phase_p3(nc, sch, io, sb, lambda: tick(W3))
        sch.barrier()
        st_num.close()
        phase_p4(nc, sch, io, sb, lambda: tick(W4), drain)
        drain(NQT)
        next(qgen)
        if "dbg_wqgu" in io:
            sch.emit("sync", lambda sy: sy.dma_start(out=io["dbg_wqgu"], in_=io["wqgu_dram"]),
                     inc="s16", k=16)
            sch.emit("sync", lambda sy: sy.dma_start(out=io["dbg_wqp"], in_=io["wqp_dram"]),
                     inc="s16", k=16)
            sch.wait("sync", "s16", sch.n("s16"))
        if "dbg_y1" in io:
            sch.emit("sync", lambda sy: sy.dma_start(out=io["dbg_y1"], in_=io["y1_dram"]),
                     inc="s16", k=16)
            sch.wait("sync", "s16", sch.n("s16"))
        if "dbg_outpre" in io:
            pass
        if "dbg_hsil" in io:
            sch.emit("sync", lambda sy: sy.dma_start(out=io["dbg_hsil"], in_=io["hsil_dram"]),
                     inc="s16", k=16)
            sch.emit("sync", lambda sy: sy.dma_start(out=io["dbg_inv"], in_=io["inv_dram"]
                     .rearrange("o (tt p) -> (o p) tt", p=128), allow_slow_non_contiguous=True),
                     inc="s16", k=16)
            sch.wait("sync", "s16", sch.n("s16"))
        sch.barrier()
        if "dbg_wqp" in io and os.environ.get("KDBG") == "wq2":
            sch.emit("sync", lambda sy: sy.dma_start(out=io["dbg_wqp"], in_=io["wqp_dram"]),
                     inc="s16", k=16)
            sch.wait("sync", "s16", sch.n("s16"))
        st_y1.close()
        st_q.close()
        phase_p5(nc, sch, io, sb)
        sch.barrier()
        if "dbg_outpre" in io:
            sch.emit("sync", lambda sy: sy.dma_start(out=io["dbg_outpre"], in_=io["outpre_dram"]),
                     inc="s16", k=16)
            sch.wait("sync", "s16", sch.n("s16"))
        phase_p6(nc, sch, io, sb)
        sch.flush(block)

    return nc


def quant_alloc(nc, stack, sb):
    FREE = 1024
    sb["q_wbf"] = [stack.enter_context(nc.sbuf_tensor(f"q_wbf{i}", [128, FREE], BF16)) for i in range(3)]
    sb["q_wq"] = [stack.enter_context(nc.sbuf_tensor(f"q_wq{i}", [128, FREE], BF16)) for i in range(2)]
    sb["q_a"] = [stack.enter_context(nc.sbuf_tensor(f"q_a{i}", [128, FREE], BF16)) for i in range(2)]
    sb["q_t"] = stack.enter_context(nc.sbuf_tensor("q_t", [128, FREE], BF16))
    sb["q_sbc"] = [stack.enter_context(nc.sbuf_tensor(f"q_sbc{i}", [128, FREE], BF16)) for i in range(2)]
    sb["q_sg"] = stack.enter_context(nc.sbuf_tensor("q_sg", [128, FREE], BF16))
    sb["q_pb"] = [stack.enter_context(nc.psum_tensor(f"q_pb{i}", [128, 512], F32)) for i in range(2)]


def quant_units(nc, sch, io, sb):
    mgrp_bf = sb["mgrp_bf"]
    w_bf, wq_sb, a_sb = sb["q_wbf"], sb["q_wq"], sb["q_a"]
    t_sb, sbc2, sg_sb = sb["q_t"], sb["q_sbc"], sb["q_sg"]
    pb = sb["q_pb"]
    FREE = 1024
    NQ = FREE // 512

    iters = []
    for hsl in range(16):
        for kc in range(KC):
            src = io["wgu_T"][kc * 128:(kc + 1) * 128, hsl * 1024:(hsl + 1) * 1024]
            dst = io["wqgu_dram"][hsl * 4:(hsl + 1) * 4, kc].rearrange("j p c -> p j c")
            iters.append((src, dst, True))
    for ch in range(64):
        for half in range(2):
            src = io["wp_T"][ch * 128:(ch + 1) * 128, half * 1024:(half + 1) * 1024]
            dst = io["wqp_dram"][ch][:, half * 1024:(half + 1) * 1024]
            iters.append((src, dst, False))
    NIT = len(iters)

    in_dma = [None] * (NIT + 4)
    a_done = [None] * (NIT + 4)
    smm = [[None] * NQ for _ in range(NIT)]
    eva = [[None] * NQ for _ in range(NIT)]
    chain_done = [None] * NIT
    out_dma = [None] * NIT

    def emit_in_dma(it):
        if it >= NIT:
            return
        src = iters[it][0]
        if it >= 3:
            sch.wait("gpsimd", "dv", chain_done[it - 3])
        in_dma[it] = sch.emit(
            "gpsimd", lambda g, d=w_bf[it % 3], s=src: g.dma_start(out=d[:], in_=s),
            inc=f"qg{it % 3}", k=16)

    emit_in_dma(0)
    emit_in_dma(1)

    two_col = sb["two_col"]
    for it, (src, dst, is_gu) in enumerate(iters):
        bi = it % 2
        bw = it % 3
        sch.wait("scalar", f"qg{bw}", in_dma[it])
        if it >= 2:
            sch.wait("scalar", "pe", smm[it - 2][NQ - 1])
            sch.wait("scalar", "dv", chain_done[it - 2])
        a_done[it] = sch.emit("scalar", lambda sc, d=a_sb[bi], s=w_bf[bw]: sc.activation(
            out=d[:], in_=s[:], func=AF.Abs, scale=two_col[:]), inc="sc")
        yield ("s1", it)

        sch.wait("tensor", "sc", a_done[it])
        for qt in range(NQ):
            if it >= 1 or qt >= 1:
                prev = eva[it][qt - 1] if qt >= 1 else eva[it - 1][NQ - 1]
                sch.wait("tensor", "dv", prev)
            smm[it][qt] = sch.emit(
                "tensor",
                lambda t, ps=pb[qt % 2], sl=a_sb[bi][:, qt * 512:(qt + 1) * 512]:
                    t.matmul(ps[:], mgrp_bf[:], sl, start=True, stop=True),
                inc="pe")
        yield ("s2", it)

        for qt in range(NQ):
            sch.wait("gpsimd", "pe", smm[it][qt])
            rly = sch.inc("gpsimd", "qe")
            sch.wait("vector", "qe", rly)
            eva[it][qt] = sch.emit(
                "vector",
                lambda v, d=sbc2[bi][:, qt * 512:(qt + 1) * 512], ps=pb[qt % 2]:
                    v.tensor_single_scalar(d, ps[:], 1e-8, ALU.max),
                inc="dv")
        sch.emit("vector", lambda v, c=sbc2[bi]: v.tensor_single_scalar(
            sg_sb[:], c[:], 2.0 ** -8, ALU.mult))
        sch.emit("vector", lambda v, a=a_sb[bi], t2=t_sb, c=sbc2[bi]: v.tensor_tensor(
            t2[:], a[:], c[:], ALU.subtract))
        sch.emit("vector", lambda v, a=a_sb[bi], t2=t_sb: v.tensor_tensor(
            a[:], t2[:], sg_sb[:], ALU.is_gt))
        sch.emit("vector", lambda v, d=sg_sb, s=w_bf[bw]: v.tensor_single_scalar(
            d[:].bitcast(U32), s[:].bitcast(U32), 0x80008000, ALU.bitwise_and))
        if it >= 2:
            sch.wait("vector", f"qs{bi}", out_dma[it - 2])
        sch.emit("vector", lambda v, a=a_sb[bi], t2=t_sb, c=sbc2[bi]: v.tensor_tensor(
            t2[:], a[:], c[:], ALU.mult))
        sch.emit("vector", lambda v, d=wq_sb[bi], t2=t_sb, g2=sg_sb: v.tensor_tensor(
            d[:].bitcast(U32), t2[:].bitcast(U32), g2[:].bitcast(U32), ALU.bitwise_or))
        chain_done[it] = sch.emit("vector", lambda v: v.engine_nop(), inc="dv")
        emit_in_dma(it + 2)
        sch.wait("sync", "dv", chain_done[it])
        oblk = dst.shape[1] if dst.ndim == 3 else 1
        out_dma[it] = sch.emit(
            "sync", lambda sy, d=dst, ss=wq_sb[bi], oblk=oblk: sy.dma_start(
                out=d,
                in_=(ss[:].rearrange("p (b c) -> p b c", b=oblk) if oblk > 1 else ss[:])),
            inc=f"qs{bi}", k=16)
        if is_gu and (it + 1) % KC == 0:
            sch.wait("sync", "qs0", sch.n("qs0"))
            sch.wait("sync", "qs1", sch.n("qs1"))
            sch.inc("sync", "qk")
        yield ("s3", it)

    sch.wait("sync", "qs0", sch.n("qs0"))
    sch.wait("sync", "qs1", sch.n("qs1"))
    while True:
        yield ("done", NIT)


def rope_elu(sch, ps, dst_bf, cos_t, sinA_t, tt, r1, r2, do_rope=True):
    if do_rope:
        cosv = cos_t[:, tt, :].unsqueeze(1).broadcast_to([128, 4, 128])
        sinv = sinA_t[:, tt, :].unsqueeze(1).broadcast_to([128, 4, 128])
        psv = ps[:].rearrange("p (h d) -> p h d", d=128)
        r1v = r1[:].rearrange("p (h d) -> p h d", d=128)
        r2v = r2[:].rearrange("p (h d) -> p h d", d=128)
        sch.emit("vector", lambda v: v.tensor_tensor(r1v, psv, cosv, ALU.mult))
        sch.emit("vector", lambda v: v.tensor_tensor(
            r2v[:, :, bass.ds(0, 64, 2)], psv[:, :, bass.ds(1, 64, 2)],
            sinv[:, :, bass.ds(0, 64, 2)], ALU.mult))
        sch.emit("vector", lambda v: v.tensor_tensor(
            r2v[:, :, bass.ds(1, 64, 2)], psv[:, :, bass.ds(0, 64, 2)],
            sinv[:, :, bass.ds(1, 64, 2)], ALU.mult))
        sch.emit("vector", lambda v: v.tensor_tensor(r1[:], r1[:], r2[:], ALU.add))
        src = r1
    else:
        sch.emit("vector", lambda v: v.tensor_copy(r1[:], ps[:]))
        src = r1
    sch.emit("vector", lambda v, s=src: v.tensor_single_scalar(r2[:], s[:], 0.0, ALU.min))
    mid = sch.emit("vector", lambda v: v.engine_nop(), inc="dv")
    sch.wait("scalar", "dv", mid)
    edone = sch.emit("scalar", lambda s: s.activation(out=r2[:], in_=r2[:], func=AF.Exp),
                     inc="sc")
    sch.wait("vector", "sc", edone)
    sch.emit("vector", lambda v, s=src, d=dst_bf: v.scalar_tensor_tensor(
        d[:], s[:], 0.0, r2[:], ALU.max, ALU.add))
    return sch.emit("vector", lambda v: v.engine_nop(), inc="dv")


def phase_p1a(nc, sch, io, sb, tick):
    with ExitStack() as ph:
        xT_bf = ph.enter_context(nc.sbuf_tensor("xT_bf", [128, KC, T], BF16))
        wslab = [ph.enter_context(nc.sbuf_tensor(f"wslab{i}", [128, KC, 512], BF16)) for i in range(2)]
        cos_t = ph.enter_context(nc.sbuf_tensor("cos_t", [128, NTT, D], F32))
        sinA_t = ph.enter_context(nc.sbuf_tensor("sinA_t", [128, NTT, D], F32))
        r1 = ph.enter_context(nc.sbuf_tensor("rope1", [128, 512], F32))
        r2 = ph.enter_context(nc.sbuf_tensor("rope2", [128, 512], F32))
        sc_bf = [ph.enter_context(nc.sbuf_tensor(f"scr_bf{i}", [128, 512], BF16)) for i in range(2)]
        ps_mm = [ph.enter_context(nc.psum_tensor(f"p1_mm{i}", [128, 512], F32)) for i in range(2)]
        ps_tr = [ph.enter_context(nc.psum_tensor(f"p1_tr{i}", [128, 128], F32)) for i in range(2)]

        qT, kT, v_sb = sb["qT"], sb["kT"], sb["v_sb"]
        ident_bf = sb["ident_bf"]

        sch.emit("gpsimd", lambda g, d=xT_bf, s=io["x_T"]: g.dma_start(
            out=d[:], in_=s.rearrange("(kc p) t -> p kc t", p=128)), inc="ld", k=16)
        sch.emit("sync", lambda sy, d=cos_t, s=io["cos_m"]: sy.dma_start(
            out=d[:], in_=s.rearrange("(tt p) d -> p tt d", p=128)), inc="ld", k=16)
        sch.emit("sync", lambda sy, d=sinA_t, s=io["sinA_m"]: sy.dma_start(
            out=d[:], in_=s.rearrange("(tt p) d -> p tt d", p=128)), inc="ld", k=16)
        ld0 = sch.n("ld")
        sch.wait("vector", "ld", ld0)
        ones_set = sch.emit("vector", lambda v, d=v_sb: v.memset(d[:, :, :, 128:129], 1.0),
                            inc="dv")

        wdma = [None] * 12
        tile_mm = {}
        tile_ev = {}
        tr_guard = {0: None, 1: None}
        pending = None

        def flush_pending():
            nonlocal pending
            if pending is None:
                return
            idx, buf, dstT, fs, tt, rope_done = pending
            last_ev = None
            for j in range(4):
                h = (fs % 4) * 4 + j
                tp = j % 2
                sch.wait("tensor", "dv", rope_done)
                sch.wait("tensor", "dv", tr_guard[tp])
                trm = sch.emit("tensor", lambda t, ps=ps_tr[tp], sbuf=sc_bf[buf], j=j:
                               t.matmul(ps[:], sbuf[:, j * 128:(j + 1) * 128], ident_bf[:],
                                        start=True, stop=True), inc="pe")
                sch.wait("vector", "pe", trm)
                last_ev = sch.emit("vector", lambda v, ps=ps_tr[tp], d=dstT, h=h, tt=tt:
                                   v.tensor_copy(d[:, h, tt * 128:(tt + 1) * 128], ps[:]),
                                   inc="dv")
                tr_guard[tp] = last_ev
            tile_ev[idx] = last_ev
            pending = None

        for fs in range(12):
            bi = fs % 2
            if fs >= 2:
                sch.wait("gpsimd", "pe", tile_mm[(fs - 2) * NTT + NTT - 1])
            wdma[fs] = sch.emit("gpsimd", lambda g, d=wslab[bi], s=io["wqkv_T"], fs=fs:
                                g.dma_start(out=d[:], in_=s[:, fs * 512:(fs + 1) * 512]
                                            .rearrange("(kc p) f -> p kc f", p=128)),
                                inc=f"ws{bi}", k=16)
            kind = "q" if fs < 4 else ("k" if fs < 8 else "v")
            for tt in range(NTT):
                idx = fs * NTT + tt
                pp = idx % 2
                sch.wait("tensor", f"ws{bi}", wdma[fs])
                if fs == 0 and tt == 0:
                    sch.wait("tensor", "ld", ld0)
                if idx - 2 >= 0:
                    sch.wait("tensor", "dv", tile_ev.get(idx - 2))
                for kc in range(KC):
                    m = sch.emit("tensor", lambda t, ps=ps_mm[pp], x=xT_bf, w=wslab[bi],
                                 kc=kc, tt=tt: t.matmul(
                                     ps[:], x[:, kc, tt * 128:(tt + 1) * 128], w[:, kc, :],
                                     start=(kc == 0), stop=(kc == KC - 1)),
                                 inc="pe" if kc == KC - 1 else None)
                    if kc == KC - 1:
                        tile_mm[idx] = m
                flush_pending()
                sch.wait("vector", "pe", tile_mm[idx])
                if kind == "v":
                    sch.wait("vector", "dv", ones_set)
                    for j in range(4):
                        h = (fs - 8) * 4 + j
                        sch.emit("vector", lambda v, ps=ps_mm[pp], tt=tt, h=h, j=j, d=v_sb:
                                 v.tensor_copy(d[:, tt, h, 0:128], ps[:, j * 128:(j + 1) * 128]))
                    tile_ev[idx] = sch.emit("vector", lambda v: v.engine_nop(), inc="dv")
                else:
                    buf = idx % 2
                    rope_done = _rope_into_flat(sch, ps_mm[pp], sc_bf[buf][:], cos_t, sinA_t,
                                                tt, r1, r2)
                    tile_ev[idx] = rope_done
                    pending = (idx, buf, qT if kind == "q" else kT, fs, tt, rope_done)
                tick()
        flush_pending()


def _rope_into_flat(sch, ps, dst_ap, cos_t, sinA_t, tt, r1, r2):
    cosv = cos_t[:, tt, :].unsqueeze(1).broadcast_to([128, 4, 128])
    sinv = sinA_t[:, tt, :].unsqueeze(1).broadcast_to([128, 4, 128])
    psv = ps[:].rearrange("p (h d) -> p h d", d=128)
    r1v = r1[:].rearrange("p (h d) -> p h d", d=128)
    r2v = r2[:].rearrange("p (h d) -> p h d", d=128)
    sch.emit("vector", lambda v: v.tensor_tensor(r1v, psv, cosv, ALU.mult))
    sch.emit("vector", lambda v: v.tensor_tensor(
        r2v[:, :, bass.ds(0, 64, 2)], psv[:, :, bass.ds(1, 64, 2)],
        sinv[:, :, bass.ds(0, 64, 2)], ALU.mult))
    sch.emit("vector", lambda v: v.tensor_tensor(
        r2v[:, :, bass.ds(1, 64, 2)], psv[:, :, bass.ds(0, 64, 2)],
        sinv[:, :, bass.ds(1, 64, 2)], ALU.mult))
    sch.emit("vector", lambda v: v.tensor_tensor(r1[:], r1[:], r2[:], ALU.add))
    sch.emit("vector", lambda v: v.tensor_single_scalar(r2[:], r1[:], 0.0, ALU.min))
    mid = sch.emit("vector", lambda v: v.engine_nop(), inc="dv")
    sch.wait("scalar", "dv", mid)
    edone = sch.emit("scalar", lambda s: s.activation(out=r2[:], in_=r2[:], func=AF.Exp),
                     inc="sc")
    sch.wait("vector", "sc", edone)
    sch.emit("vector", lambda v, d=dst_ap: v.scalar_tensor_tensor(
        d, r1[:], 0.0, r2[:], ALU.max, ALU.add))
    return sch.emit("vector", lambda v: v.engine_nop(), inc="dv")


def phase_p1b(nc, sch, io, sb, tick):
    with ExitStack() as ph:
        xaT_bf = ph.enter_context(nc.sbuf_tensor("xaT_bf", [128, KC, T], BF16))
        wslab = ph.enter_context(nc.sbuf_tensor("waslab", [128, KC, 512], BF16))
        cos_t = ph.enter_context(nc.sbuf_tensor("cos_ta", [128, NTT, D], F32))
        sinA_t = ph.enter_context(nc.sbuf_tensor("sinA_ta", [128, NTT, D], F32))
        auxo = ph.enter_context(nc.sbuf_tensor("auxo_bf", [128, NTT], BF16))
        r1 = ph.enter_context(nc.sbuf_tensor("arope1", [128, 512], F32))
        r2 = ph.enter_context(nc.sbuf_tensor("arope2", [128, 512], F32))
        kaux = ph.enter_context(nc.sbuf_tensor("kaux", [128, NTT, 512], BF16))
        vaug = ph.enter_context(nc.sbuf_tensor("vaug", [128, 4, 129], BF16))
        ps_mm = [ph.enter_context(nc.psum_tensor(f"pa_mm{i}", [128, 512], F32)) for i in range(2)]
        ps_sa = [ph.enter_context(nc.psum_tensor(f"pa_sa{i}", [128, 129], F32)) for i in range(4)]

        S_init = sb["S_init"]

        sch.emit("gpsimd", lambda g, d=xaT_bf, s=io["xaux_T"]: g.dma_start(
            out=d[:], in_=s.rearrange("(kc p) t -> p kc t", p=128)), inc="ld", k=16)
        sch.emit("gpsimd", lambda g, d=auxo, s=io["auxones"]: g.dma_start(
            out=d[:], in_=s.rearrange("(tt p) c -> p (tt c)", p=128),
            allow_slow_non_contiguous=True), inc="ld", k=16)
        sch.emit("sync", lambda sy, d=cos_t, s=io["cos_a"]: sy.dma_start(
            out=d[:], in_=s.rearrange("(tt p) d -> p tt d", p=128)), inc="ld", k=16)
        sch.emit("sync", lambda sy, d=sinA_t, s=io["sinA_a"]: sy.dma_start(
            out=d[:], in_=s.rearrange("(tt p) d -> p tt d", p=128)), inc="ld", k=16)
        lds = sch.n("ld")
        sch.wait("vector", "ld", lds)

        prev_kmm_last = None
        prev_vmm_last = None
        prev_smm_last = None
        for i in range(4):
            if prev_vmm_last is not None:
                sch.wait("gpsimd", "pe", prev_vmm_last)
            wk = sch.emit("gpsimd", lambda g, d=wslab, s=io["wqkv_T"], i=i: g.dma_start(
                out=d[:], in_=s[:, (4 + i) * 512:(5 + i) * 512].rearrange("(kc p) f -> p kc f", p=128)),
                inc="ws0", k=16)
            kev = [None] * NTT
            kmm = [None] * NTT
            for tt in range(NTT):
                pp = tt % 2
                sch.wait("tensor", "ws0", wk)
                if i == 0 and tt == 0:
                    sch.wait("tensor", "ld", lds)
                if tt >= 2:
                    sch.wait("tensor", "dv", kev[tt - 2])
                for kc in range(KC):
                    m = sch.emit("tensor", lambda t, ps=ps_mm[pp], x=xaT_bf, w=wslab,
                                 kc=kc, tt=tt: t.matmul(
                                     ps[:], x[:, kc, tt * 128:(tt + 1) * 128], w[:, kc, :],
                                     start=(kc == 0), stop=(kc == KC - 1)),
                                 inc="pe" if kc == KC - 1 else None)
                    if kc == KC - 1:
                        kmm[tt] = m
                sch.wait("vector", "pe", kmm[tt])
                if prev_smm_last is not None and tt == 0:
                    sch.wait("vector", "pe", prev_smm_last)
                kev[tt] = _rope_into_flat(sch, ps_mm[pp], kaux[:, tt, :], cos_t, sinA_t,
                                          tt, r1, r2)
                tick()
            prev_kmm_last = kmm[NTT - 1]
            sch.wait("gpsimd", "pe", prev_kmm_last)
            wv = sch.emit("gpsimd", lambda g, d=wslab, s=io["wqkv_T"], i=i: g.dma_start(
                out=d[:], in_=s[:, (8 + i) * 512:(9 + i) * 512].rearrange("(kc p) f -> p kc f", p=128)),
                inc="ws0", k=16)
            smm = [None] * NTT
            vev = [None] * NTT
            for tt in range(NTT):
                pp = tt % 2
                sch.wait("tensor", "ws0", wv)
                sch.wait("tensor", "dv", kev[tt])
                if tt >= 2:
                    sch.wait("tensor", "dv", vev[tt - 2])
                vmm = None
                for kc in range(KC):
                    m = sch.emit("tensor", lambda t, ps=ps_mm[pp], x=xaT_bf, w=wslab,
                                 kc=kc, tt=tt: t.matmul(
                                     ps[:], x[:, kc, tt * 128:(tt + 1) * 128], w[:, kc, :],
                                     start=(kc == 0), stop=(kc == KC - 1)),
                                 inc="pe" if kc == KC - 1 else None)
                    if kc == KC - 1:
                        vmm = m
                prev_vmm_last = vmm
                sch.wait("vector", "pe", vmm)
                if tt >= 1:
                    sch.wait("vector", "pe", smm[tt - 1])
                for j in range(4):
                    sch.emit("vector", lambda v, ps=ps_mm[pp], j=j, d=vaug:
                             v.tensor_copy(d[:, j, 0:128], ps[:, j * 128:(j + 1) * 128]))
                sch.emit("vector", lambda v, d=vaug, s=auxo, tt=tt: v.tensor_copy(
                    d[:, :, 128:129], s[:, tt:tt + 1].unsqueeze(1).broadcast_to([128, 4, 1])))
                vev[tt] = sch.emit("vector", lambda v: v.engine_nop(), inc="dv")
                sch.wait("tensor", "dv", vev[tt])
                for j in range(4):
                    m = sch.emit("tensor", lambda t, ps=ps_sa[j], ka=kaux, va=vaug, j=j, tt=tt:
                                 t.matmul(ps[:], ka[:, tt, j * 128:(j + 1) * 128], va[:, j, :],
                                          start=(tt == 0), stop=(tt == NTT - 1),
                                          skip_group_check=True),
                                 inc="pe" if j == 3 else None)
                    if j == 3:
                        smm[tt] = m
                tick()
            sch.wait("vector", "pe", smm[NTT - 1])
            for j in range(4):
                sch.emit("vector", lambda v, ps=ps_sa[j], h=4 * i + j, d=S_init:
                         v.tensor_copy(d[:, h, :], ps[:]))
            sed = sch.emit("vector", lambda v: v.engine_nop(), inc="dv")
            sch.wait("tensor", "dv", sed)
            prev_smm_last = smm[NTT - 1]


def phase_p2(nc, sch, io, sb, tick):
    NU = H * NCH
    with ExitStack() as ph:
        attT = [ph.enter_context(nc.sbuf_tensor(f"attT{i}", [128, 2 * CH], BF16)) for i in range(2)]
        ktok2 = [ph.enter_context(nc.sbuf_tensor(f"ktok{i}", [128, 2, 128], BF16)) for i in range(2)]
        nsc = [ph.enter_context(nc.sbuf_tensor(f"nsc{i}", [128, 256], BF16)) for i in range(2)]
        denb = [ph.enter_context(nc.sbuf_tensor(f"denb{i}", [128, 2], F32)) for i in range(2)]
        Sf = ph.enter_context(nc.sbuf_tensor("Sf", [128, H, 129], F32))
        Sbf = ph.enter_context(nc.sbuf_tensor("Sbf", [128, H, 129], BF16))
        ps_att = [ph.enter_context(nc.psum_tensor(f"p2_at{i}", [128, 512], F32)) for i in range(2)]
        ps_num = [ph.enter_context(nc.psum_tensor(f"p2_nm{i}", [128, 512], F32)) for i in range(2)]
        ps_S = [ph.enter_context(nc.psum_tensor(f"p2_S{i}", [128, 512], F32)) for i in range(2)]

        qT, kT, v_sb, S_init, numT = sb["qT"], sb["kT"], sb["v_sb"], sb["S_init"], sb["numT"]
        trilm_bf, ident_bf = sb["trilm_bf"], sb["ident_bf"]

        sch.emit("vector", lambda v: v.tensor_copy(Sf[:], S_init[:]))
        sbf0 = sch.emit("vector", lambda v: v.tensor_copy(Sbf[:], Sf[:]), inc="dv")

        ktr_last = [None] * NU
        amm_last = [None] * NU
        nmm_last = [None] * NU
        summ_last = [None] * NU
        trm_last = [None] * NU
        ktok_ok = [None] * NU
        att_ok = [None] * NU
        nok = [None] * NU
        numTev_ok = [None] * NU
        Sevac_ok = [None] * NU
        sbf_mark = [sbf0] * H

        def emit_ktr(u):
            if u >= NU:
                return
            c, hh = divmod(u, H)
            B = u % 2
            if u >= 2:
                sch.wait("tensor", "dv", Sevac_ok[u - 2])
            m = None
            for mh in range(2):
                m = sch.emit("tensor", lambda t, B=B, mh=mh, c=c, hh=hh: t.matmul(
                    ps_S[B][:, 256 + mh * 128:384 + mh * 128],
                    kT[:, hh, c * 256 + mh * 128:c * 256 + mh * 128 + 128],
                    ident_bf[:], start=True, stop=True, skip_group_check=True),
                    inc="pe" if mh == 1 else None)
            ktr_last[u] = m

        def emit_amm(u):
            if u >= NU:
                return
            c, hh = divmod(u, H)
            B = u % 2
            if u >= 2:
                sch.wait("tensor", "dv", numTev_ok[u - 2])
            m = None
            for mh in range(2):
                m = sch.emit("tensor", lambda t, B=B, mh=mh, c=c, hh=hh: t.matmul(
                    ps_att[B][:, mh * 256:(mh + 1) * 256],
                    kT[:, hh, c * 256 + mh * 128:c * 256 + mh * 128 + 128],
                    qT[:, hh, c * 256:(c + 1) * 256],
                    start=True, stop=True, skip_group_check=True),
                    inc="pe" if mh == 1 else None)
            amm_last[u] = m

        def emit_ktokev(u):
            if u >= NU:
                return
            B = u % 2
            sch.wait("vector", "pe", ktr_last[u])
            if u >= 2:
                sch.wait("vector", "pe", summ_last[u - 2])
            for mh in range(2):
                sch.emit("vector", lambda v, B=B, mh=mh: v.tensor_copy(
                    ktok2[B][:, mh, :], ps_S[B][:, 256 + mh * 128:384 + mh * 128]))
            ktok_ok[u] = sch.emit("vector", lambda v: v.engine_nop(), inc="dv")

        def emit_attev(u):
            if u >= NU:
                return
            B = u % 2
            sch.wait("vector", "pe", amm_last[u])
            if u >= 2:
                sch.wait("vector", "pe", nmm_last[u - 2])
            sch.emit("vector", lambda v, B=B: v.tensor_tensor(
                attT[B][:], ps_att[B][:], trilm_bf[:], ALU.mult))
            att_ok[u] = sch.emit("vector", lambda v: v.engine_nop(), inc="dv")

        emit_ktr(0)
        emit_amm(0)
        emit_ktokev(0)
        emit_attev(0)

        for u in range(NU):
            c, hh = divmod(u, H)
            B = u % 2
            emit_ktr(u + 1)
            emit_amm(u + 1)
            emit_ktokev(u + 1)
            emit_attev(u + 1)
            sch.wait("tensor", "dv", att_ok[u])
            sch.wait("tensor", "dv", sbf_mark[hh])
            first = True
            m = None
            for lh in range(2):
                for mh in range(2):
                    m = sch.emit("tensor", lambda t, B=B, lh=lh, mh=mh, c=c, hh=hh, st=first:
                                 t.matmul(ps_num[B][:, lh * 256:lh * 256 + 129],
                                          attT[B][:, mh * 256 + lh * 128:mh * 256 + lh * 128 + 128],
                                          v_sb[:, c * 2 + mh, hh, :],
                                          start=st, stop=False, skip_group_check=True))
                    first = False
                m = sch.emit("tensor", lambda t, B=B, lh=lh, c=c, hh=hh: t.matmul(
                    ps_num[B][:, lh * 256:lh * 256 + 129],
                    qT[:, hh, c * 256 + lh * 128:c * 256 + lh * 128 + 128],
                    Sbf[:, hh, :], start=False, stop=(lh == 1), skip_group_check=True),
                    inc="pe" if lh == 1 else None)
            nmm_last[u] = m
            sch.wait("tensor", "dv", ktok_ok[u])
            if u >= 2:
                sch.wait("tensor", "dv", Sevac_ok[u - 2])
            m = None
            for mh in range(2):
                m = sch.emit("tensor", lambda t, B=B, mh=mh, c=c, hh=hh: t.matmul(
                    ps_S[B][:, 0:129], ktok2[B][:, mh, :], v_sb[:, c * 2 + mh, hh, :],
                    start=(mh == 0), stop=(mh == 1), skip_group_check=True),
                    inc="pe" if mh == 1 else None)
            summ_last[u] = m
            sch.wait("vector", "pe", nmm_last[u])
            f1 = sch.emit("vector", lambda v, B=B: v.tensor_single_scalar(
                denb[B][:], ps_num[B][:, bass.ds(128, 2, 256)], 1e-6, ALU.max), inc="dv")
            sch.wait("vector", "dv", f1)
            f2 = sch.emit("vector", lambda v, B=B: v.reciprocal(denb[B][:], denb[B][:]),
                          inc="dv")
            sch.wait("vector", "dv", f2)
            for lh in range(2):
                sch.emit("vector", lambda v, B=B, lh=lh: v.tensor_tensor(
                    nsc[B][:, lh * 128:(lh + 1) * 128], ps_num[B][:, lh * 256:lh * 256 + 128],
                    denb[B][:, lh:lh + 1].broadcast_to([128, 128]), ALU.mult))
            nok[u] = sch.emit("vector", lambda v: v.engine_nop(), inc="dv")
            sch.wait("vector", "pe", summ_last[u])
            sch.emit("vector", lambda v, B=B, hh=hh: v.tensor_tensor(
                Sf[:, hh, :], Sf[:, hh, :], ps_S[B][:, 0:129], ALU.add))
            Sevac_ok[u] = sch.emit("vector", lambda v, hh=hh: v.tensor_copy(
                Sbf[:, hh, :], Sf[:, hh, :]), inc="dv")
            sbf_mark[hh] = Sevac_ok[u]
            sch.wait("tensor", "dv", nok[u])
            m = None
            for lh in range(2):
                m = sch.emit("tensor", lambda t, B=B, lh=lh: t.matmul(
                    ps_att[B][:, lh * 128:(lh + 1) * 128], nsc[B][:, lh * 128:(lh + 1) * 128],
                    ident_bf[:], start=True, stop=True, skip_group_check=True),
                    inc="pe" if lh == 1 else None)
            trm_last[u] = m
            sch.wait("vector", "pe", trm_last[u])
            for lh in range(2):
                mk = sch.emit("vector", lambda v, B=B, lh=lh, c=c, hh=hh: v.tensor_copy(
                    numT[:, hh, c * 256 + lh * 128:c * 256 + lh * 128 + 128],
                    ps_att[B][:, lh * 128:(lh + 1) * 128]), inc="dv" if lh == 1 else None)
            numTev_ok[u] = mk
            tick()


def phase_p3(nc, sch, io, sb, tick):
    with ExitStack() as ph:
        wo_bf = ph.enter_context(nc.sbuf_tensor("wo_bf", [128, KC, E], BF16))
        xtok = [ph.enter_context(nc.sbuf_tensor(f"xtok{i}", [128, E], F32)) for i in range(2)]
        y1f = ph.enter_context(nc.sbuf_tensor("y1f", [128, E], F32))
        sq = ph.enter_context(nc.sbuf_tensor("sq3", [128, E], F32))
        stat = ph.enter_context(nc.sbuf_tensor("stat3", [128, 4], F32))
        y1bf = [ph.enter_context(nc.sbuf_tensor(f"y1bf{i}", [128, E], BF16)) for i in range(2)]
        ps_mm = [ph.enter_context(nc.psum_tensor(f"p3_mm{i}", [128, 512], F32)) for i in range(4)]
        ps_tr = [ph.enter_context(nc.psum_tensor(f"p3_tr{i}", [128, 128], F32)) for i in range(2)]

        numT, y1T, ident_bf = sb["numT"], sb["y1T"], sb["ident_bf"]

        wo = sch.emit("gpsimd", lambda g, d=wo_bf, s=io["wout_T"]: g.dma_start(
            out=d[:], in_=s.rearrange("(kc p) f -> p kc f", p=128)), inc="ld", k=16)

        xd = [None] * NTT
        for tt in range(2):
            xd[tt] = sch.emit("sync", lambda sy, d=xtok[tt % 2], s=io["x_tok"], tt=tt:
                              sy.dma_start(out=d[:], in_=s[tt * 128:(tt + 1) * 128, :]),
                              inc=f"xk{tt % 2}", k=16)

        tr_guard = {0: None, 1: None}
        res_done = [None] * NTT
        prev_y1_store = None
        last_trm = None
        pending = None

        def flush_pending():
            nonlocal pending, last_trm
            if pending is None:
                return
            ptt, buf, y1ok = pending
            for kc in range(KC):
                tp = kc % 2
                sch.wait("tensor", "dv", y1ok)
                sch.wait("tensor", "dv", tr_guard[tp])
                trm = sch.emit("tensor", lambda t, ps=ps_tr[tp], ybf=y1bf[buf], kc=kc: t.matmul(
                    ps[:], ybf[:, kc * 128:(kc + 1) * 128], ident_bf[:],
                    start=True, stop=True), inc="pe")
                last_trm = trm
                sch.wait("vector", "pe", trm)
                tr_guard[tp] = sch.emit(
                    "vector", lambda v, ps=ps_tr[tp], kc=kc, ptt=ptt: v.tensor_copy(
                        y1T[:, kc, ptt * 128:(ptt + 1) * 128], ps[:]), inc="dv")
            pending = None

        for tt in range(NTT):
            sch.wait("tensor", "ld", wo)
            if tt >= 1:
                sch.wait("tensor", "dv", res_done[tt - 1])
            pmm = None
            for h in range(H):
                for os in range(4):
                    pmm = sch.emit("tensor", lambda t, h=h, os=os, tt=tt: t.matmul(
                        ps_mm[os][:], numT[:, h, tt * 128:(tt + 1) * 128],
                        wo_bf[:, h, os * 512:(os + 1) * 512],
                        start=(h == 0), stop=(h == H - 1), skip_group_check=True),
                        inc="pe" if (h == H - 1 and os == 3) else None)
            flush_pending()
            sch.wait("vector", "pe", pmm)
            sch.wait("vector", f"xk{tt % 2}", xd[tt])
            if prev_y1_store is not None:
                sch.wait("vector", "y1s", prev_y1_store)
            for os in range(4):
                sch.emit("vector", lambda v, os=os, tt=tt: v.tensor_tensor(
                    y1f[:, os * 512:(os + 1) * 512], ps_mm[os][:],
                    xtok[tt % 2][:, os * 512:(os + 1) * 512], ALU.add))
            res_done[tt] = sch.emit("vector", lambda v: v.engine_nop(), inc="dv")
            if tt + 2 < NTT:
                sch.wait("sync", "dv", res_done[tt])
                xd[tt + 2] = sch.emit("sync", lambda sy, d=xtok[tt % 2], s=io["x_tok"], tt=tt:
                                      sy.dma_start(out=d[:], in_=s[(tt + 2) * 128:(tt + 3) * 128, :]),
                                      inc=f"xk{tt % 2}", k=16)
            sch.emit("vector", lambda v: v.tensor_tensor(sq[:], y1f[:], y1f[:], ALU.mult))
            sch.emit("vector", lambda v: v.reduce_sum(stat[:, 0:1], y1f[:], AX.X))
            sch.emit("vector", lambda v: v.reduce_sum(stat[:, 1:2], sq[:], AX.X))
            fm = sch.emit("vector", lambda v: v.tensor_single_scalar(stat[:, 0:1], stat[:, 0:1],
                                                                     1.0 / E, ALU.mult), inc="dv")
            sch.wait("vector", "dv", fm)
            sch.emit("vector", lambda v: v.tensor_single_scalar(stat[:, 1:2], stat[:, 1:2],
                                                                1.0 / E, ALU.mult))
            sch.emit("vector", lambda v: v.tensor_tensor(stat[:, 2:3], stat[:, 0:1],
                                                         stat[:, 0:1], ALU.mult))
            sch.emit("vector", lambda v: v.tensor_tensor(stat[:, 1:2], stat[:, 1:2],
                                                         stat[:, 2:3], ALU.subtract))
            vst = sch.emit("vector", lambda v: v.tensor_single_scalar(
                stat[:, 1:2], stat[:, 1:2], LN_EPS, ALU.add), inc="dv")
            sch.wait("vector", "dv", vst)
            sch.wait("scalar", "dv", vst)
            sqd = sch.emit("scalar", lambda s: s.activation(out=stat[:, 3:4], in_=stat[:, 1:2],
                                                            func=AF.Sqrt), inc="sc")
            sch.wait("vector", "sc", sqd)
            fr = sch.emit("vector", lambda v: v.reciprocal(stat[:, 3:4], stat[:, 3:4]), inc="dv")
            sch.wait("vector", "dv", fr)
            sch.emit("vector", lambda v: v.tensor_tensor(
                y1f[:], y1f[:], stat[:, 0:1].broadcast_to([128, E]), ALU.subtract))
            sch.emit("vector", lambda v: v.tensor_tensor(
                y1f[:], y1f[:], stat[:, 3:4].broadcast_to([128, E]), ALU.mult))
            if last_trm is not None:
                sch.wait("vector", "pe", last_trm)
            sch.emit("vector", lambda v, d=y1bf[tt % 2]: v.tensor_copy(d[:], y1f[:]))
            y1ok = sch.emit("vector", lambda v: v.engine_nop(), inc="dv")
            sch.wait("sync", "dv", y1ok)
            prev_y1_store = sch.emit("sync", lambda sy, tt=tt: sy.dma_start(
                out=io["y1_dram"][tt * 128:(tt + 1) * 128, :], in_=y1f[:]), inc="y1s", k=16)
            pending = (tt, tt % 2, y1ok)
            tick()
        flush_pending()
        sch.wait("sync", "y1s", sch.n("y1s"))
        sch.wait("sync", "xk0", sch.n("xk0"))
        sch.wait("sync", "xk1", sch.n("xk1"))


def phase_p4(nc, sch, io, sb, tick, drain):
    NWB = 4
    with ExitStack() as ph:
        wgu = [ph.enter_context(nc.sbuf_tensor(f"wgu{i}", [128, KC, 256], BF16)) for i in range(NWB)]
        silu_f = ph.enter_context(nc.sbuf_tensor("silu_f", [128, 512], F32))
        hsil = [ph.enter_context(nc.sbuf_tensor(f"hsil{i}", [128, T], BF16)) for i in range(2)]
        sqb = [ph.enter_context(nc.sbuf_tensor(f"sqb{i}", [128, 512], BF16)) for i in range(2)]
        ssqrow = ph.enter_context(nc.sbuf_tensor("ssqrow", [1, T], F32))
        invrow = ph.enter_context(nc.sbuf_tensor("invrow", [1, T], F32))
        ps_g = [ph.enter_context(nc.psum_tensor(f"p4_g{i}", [128, 512], F32)) for i in range(2)]
        ps_u = [ph.enter_context(nc.psum_tensor(f"p4_u{i}", [128, 512], F32)) for i in range(2)]
        ps_q = [ph.enter_context(nc.psum_tensor(f"p4_q{i}", [128, 512], F32)) for i in range(2)]

        y1T, onescol_bf, inv_sb = sb["y1T"], sb["onescol_bf"], sb["inv_sb"]

        wdma = [None] * NJJ
        umm_last = [None] * NJJ
        hs_dma = [None] * NJJ
        hdone_list = []
        ssq_done = [None] * (2 * NJJ)
        sq_pend = None
        prev_hdone = None

        def emit_ssq():
            nonlocal sq_pend
            if sq_pend is None:
                return
            psq, pjj, pth, pgi = sq_pend
            sch.wait("tensor", "sc", psq)
            ssq_done[pgi] = sch.emit("tensor", lambda t, ps=ps_q[pth], sq=sqb[pgi % 2],
                                     pjj=pjj: t.matmul(
                ps[0:1, :], onescol_bf[:], sq[:], start=(pjj == 0), stop=(pjj == NJJ - 1),
                skip_group_check=True), inc="pe")
            sq_pend = None

        def emit_wdma(jj):
            if jj >= NJJ:
                return
            ws = jj % NWB
            drain((jj // 4 + 1) * KC * 3)
            sch.wait("scalar", "qk", jj // 4 + 1)
            if jj >= NWB:
                sch.wait("scalar", "pe", umm_last[jj - NWB])
            wdma[jj] = sch.emit("scalar", lambda g, d=wgu[ws], s=io["wqgu_dram"], jj=jj:
                                g.dma_start(out=d[:], in_=s[jj].rearrange("kc p c -> p kc c")),
                                inc=f"wg{ws}", k=16)

        emit_wdma(0)
        emit_wdma(1)
        for jj in range(NJJ):
            wslot = jj % NWB
            hslot = jj % 2
            emit_wdma(jj + 2)
            if jj >= 2:
                sch.wait("vector", f"hs{hslot}", hs_dma[jj - 2])
            for th in range(2):
                gu_idx = jj * 2 + th
                pp = gu_idx % 2
                sch.wait("tensor", f"wg{wslot}", wdma[jj])
                if gu_idx >= 2:
                    sch.wait("tensor", "dv", hdone_list[gu_idx - 2])
                umm = None
                for kc in range(KC):
                    sch.emit("tensor", lambda t, ps=ps_g[pp], w=wgu[wslot], kc=kc, th=th:
                             t.matmul(ps[:], w[:, kc, 0:128], y1T[:, kc, th * 512:(th + 1) * 512],
                                      start=(kc == 0), stop=(kc == KC - 1), skip_group_check=True))
                for kc in range(KC):
                    m = sch.emit("tensor", lambda t, ps=ps_u[pp], w=wgu[wslot], kc=kc, th=th:
                                 t.matmul(ps[:], w[:, kc, 128:256], y1T[:, kc, th * 512:(th + 1) * 512],
                                          start=(kc == 0), stop=(kc == KC - 1), skip_group_check=True),
                                 inc="pe" if kc == KC - 1 else None)
                    if kc == KC - 1:
                        umm = m
                if th == 1:
                    umm_last[jj] = umm
                emit_ssq()
                sch.wait("scalar", "pe", umm)
                if prev_hdone is not None:
                    sch.wait("scalar", "dv", prev_hdone)
                sdone = sch.emit("scalar", lambda s, ps=ps_g[pp]: s.activation(
                    out=silu_f[:], in_=ps[:], func=AF.Silu), inc="sc")
                sch.wait("vector", "sc", sdone)
                hdone = sch.emit("vector", lambda v, ps=ps_u[pp], d=hsil[hslot], th=th:
                                 v.tensor_tensor(d[:, th * 512:(th + 1) * 512], silu_f[:],
                                                 ps[:], ALU.mult), inc="dv")
                prev_hdone = hdone
                hdone_list.append(hdone)
                sch.wait("scalar", "dv", hdone)
                if gu_idx >= 2:
                    sch.wait("scalar", "pe", ssq_done[gu_idx - 2])
                sq_done = sch.emit("scalar", lambda s, d=sqb[gu_idx % 2], src=hsil[hslot],
                                   th=th: s.activation(
                    out=d[:], in_=src[:, th * 512:(th + 1) * 512], func=AF.Square), inc="sc")
                sq_pend = (sq_done, jj, th, gu_idx)
                tick()
            sch.wait("sync", "dv", hdone_list[jj * 2 + 1])
            hs_dma[jj] = sch.emit("sync", lambda sy, s=hsil[hslot], jj=jj: sy.dma_start(
                out=io["hsil_dram"][jj], in_=s[:]), inc=f"hs{hslot}", k=16)
        emit_ssq()

        sch.wait("vector", "pe", ssq_done[2 * NJJ - 1])
        sch.emit("vector", lambda v: v.tensor_copy(ssqrow[:, 0:512], ps_q[0][0:1, :]))
        sch.emit("vector", lambda v: v.tensor_copy(ssqrow[:, 512:1024], ps_q[1][0:1, :]))
        sch.emit("vector", lambda v: v.tensor_scalar(ssqrow[:], ssqrow[:], 1.0 / HP, RMS_EPS,
                                                     ALU.mult, ALU.add))
        vr = sch.emit("vector", lambda v: v.engine_nop(), inc="dv")
        sch.wait("scalar", "dv", vr)
        sqd = sch.emit("scalar", lambda s: s.activation(out=invrow[:], in_=ssqrow[:],
                                                        func=AF.Sqrt), inc="sc")
        sch.wait("vector", "sc", sqd)
        rd = sch.emit("vector", lambda v: v.reciprocal(invrow[:], invrow[:]), inc="dv")
        sch.wait("sync", "dv", rd)
        st1 = sch.emit("sync", lambda sy: sy.dma_start(
            out=io["inv_dram"], in_=invrow[:]), inc="inv", k=16)
        sch.wait("sync", "inv", st1)
        sch.emit("sync", lambda sy: sy.dma_start(
            out=inv_sb[:], in_=io["inv_dram"].rearrange("o (tt p) -> (o p) tt", p=128),
            allow_slow_non_contiguous=True), inc="inv", k=16)
        sch.wait("sync", "inv", sch.n("inv"))
        sch.wait("sync", "hs0", sch.n("hs0"))
        sch.wait("sync", "hs1", sch.n("hs1"))


def phase_p5(nc, sch, io, sb):
    with ExitStack() as ph:
        wqp = ph.enter_context(nc.sbuf_tensor("wqp_sb", [128, 64, 512], BF16))
        hs = [ph.enter_context(nc.sbuf_tensor(f"p5h{i}", [128, 4 * T], BF16)) for i in range(2)]
        y1dg = ph.enter_context(nc.sbuf_tensor("y1dg", [128, NTT, 512], F32))
        op = ph.enter_context(nc.sbuf_tensor("op5", [128, NTT, 512], F32))
        ps_o = [ph.enter_context(nc.psum_tensor(f"p5_o{i}", [128, 512], F32)) for i in range(8)]

        inv_sb = sb["inv_sb"]

        prev_ev = None
        prev_store = None
        k_mm = {}
        for dg in range(4):
            if dg > 0:
                sch.wait("gpsimd", "pe", k_mm[(dg - 1, 15)])
            wd = None
            for part in range(4):
                wd = sch.emit("gpsimd", lambda g, d=wqp, s=io["wqp_dram"], dg=dg, part=part:
                              g.dma_start(out=d[:, part * 16:(part + 1) * 16, :],
                                          in_=s[part * 16:(part + 1) * 16, :, dg * 512:(dg + 1) * 512]
                                          .rearrange("k p c -> p k c")),
                              inc="p5w", k=16)
            if prev_ev is not None:
                sch.wait("sync", "dv", prev_ev)
            yd = sch.emit("sync", lambda sy, d=y1dg, s=io["y1_dram"], dg=dg: sy.dma_start(
                out=d[:], in_=s[:, dg * 512:(dg + 1) * 512].rearrange("(tt p) c -> p tt c", p=128)),
                inc="p5y", k=16)
            for kb in range(16):
                hb = kb % 2
                if (dg, kb - 2) in k_mm:
                    sch.wait("scalar", "pe", k_mm[(dg, kb - 2)])
                elif dg > 0 and kb < 2:
                    sch.wait("scalar", "pe", k_mm[(dg - 1, 14 + kb)])
                hd = sch.emit("scalar", lambda s, d=hs[hb], kb=kb: s.dma_start(
                    out=d[:].rearrange("p (c t) -> p c t", c=4),
                    in_=io["hsil_dram"][kb * 4:(kb + 1) * 4].rearrange("c p t -> p c t")),
                    inc=f"ha{hb}", k=16)
                sch.wait("tensor", f"ha{hb}", hd)
                sch.wait("tensor", "p5w", wd)
                if dg > 0 and kb == 0:
                    sch.wait("tensor", "dv", prev_ev)
                for kk in range(4):
                    k = kb * 4 + kk
                    for tt in range(NTT):
                        k_mm[(dg, kb)] = sch.emit("tensor", lambda t, k=k, kk=kk, tt=tt, hb=hb:
                            t.matmul(ps_o[tt][:], hs[hb][:, kk * T + tt * 128:kk * T + (tt + 1) * 128],
                                     wqp[:, k, :],
                                     start=(k == 0), stop=(k == 63), skip_group_check=True),
                            inc="pe" if tt == NTT - 1 else None)
            sch.wait("vector", "pe", k_mm[(dg, 15)])
            sch.wait("vector", "p5y", yd)
            if prev_store is not None:
                sch.wait("vector", "p5s", prev_store)
            for tt in range(NTT):
                sch.emit("vector", lambda v, tt=tt: v.scalar_tensor_tensor(
                    op[:, tt, :], ps_o[tt][:], inv_sb[:, tt:tt + 1], y1dg[:, tt, :],
                    ALU.mult, ALU.add))
            prev_ev = sch.emit("vector", lambda v: v.engine_nop(), inc="dv")
            sch.wait("sync", "dv", prev_ev)
            prev_store = sch.emit("sync", lambda sy, dg=dg: sy.dma_start(
                out=io["outpre_dram"][:, dg * 512:(dg + 1) * 512]
                    .rearrange("(tt p) c -> p tt c", p=128), in_=op[:]), inc="p5s", k=16)
        sch.wait("sync", "p5s", sch.n("p5s"))


def phase_p6(nc, sch, io, sb):
    with ExitStack() as ph:
        row = [ph.enter_context(nc.sbuf_tensor(f"p6r{i}", [128, E], F32)) for i in range(2)]
        sq = [ph.enter_context(nc.sbuf_tensor(f"sq6{i}", [128, E], F32)) for i in range(2)]
        stat = ph.enter_context(nc.sbuf_tensor("stat6", [128, 4], F32))
        orow = [ph.enter_context(nc.sbuf_tensor(f"p6o{i}", [128, E], F32)) for i in range(2)]

        ld = [None] * NTT
        std = [None] * NTT
        sq_used = [None] * NTT
        for tt in range(2):
            ld[tt] = sch.emit("sync", lambda sy, d=row[tt % 2], tt=tt: sy.dma_start(
                out=d[:], in_=io["outpre_dram"][tt * 128:(tt + 1) * 128, :]), inc=f"o6l{tt % 2}", k=16)
        for tt in range(NTT):
            bi = tt % 2
            sch.wait("vector", f"o6l{bi}", ld[tt])
            if tt >= 2:
                sch.wait("vector", f"o6s{bi}", std[tt - 2])
            sch.wait("scalar", f"o6l{bi}", ld[tt])
            if tt >= 2:
                sch.wait("scalar", "dv", sq_used[tt - 2])
            sqd0 = sch.emit("scalar", lambda sc, r=row[bi], q=sq[bi]: sc.activation(
                out=q[:], in_=r[:], func=AF.Square), inc="sc")
            sch.emit("vector", lambda v, r=row[bi]: v.reduce_sum(stat[:, 0:1], r[:], AX.X))
            sch.wait("vector", "sc", sqd0)
            sq_used[tt] = sch.emit("vector", lambda v, q=sq[bi]: v.reduce_sum(
                stat[:, 1:2], q[:], AX.X), inc="dv")
            fm = sch.emit("vector", lambda v: v.tensor_single_scalar(stat[:, 0:1], stat[:, 0:1],
                                                                     1.0 / E, ALU.mult), inc="dv")
            sch.wait("vector", "dv", fm)
            sch.emit("vector", lambda v: v.tensor_single_scalar(stat[:, 1:2], stat[:, 1:2],
                                                                1.0 / E, ALU.mult))
            sch.emit("vector", lambda v: v.tensor_tensor(stat[:, 2:3], stat[:, 0:1],
                                                         stat[:, 0:1], ALU.mult))
            sch.emit("vector", lambda v: v.tensor_tensor(stat[:, 1:2], stat[:, 1:2],
                                                         stat[:, 2:3], ALU.subtract))
            vst = sch.emit("vector", lambda v: v.tensor_single_scalar(
                stat[:, 1:2], stat[:, 1:2], LN_EPS, ALU.add), inc="dv")
            sch.wait("vector", "dv", vst)
            sch.wait("scalar", "dv", vst)
            sqd = sch.emit("scalar", lambda s: s.activation(out=stat[:, 3:4], in_=stat[:, 1:2],
                                                            func=AF.Sqrt), inc="sc")
            sch.wait("vector", "sc", sqd)
            fr = sch.emit("vector", lambda v: v.reciprocal(stat[:, 3:4], stat[:, 3:4]), inc="dv")
            sch.wait("vector", "dv", fr)
            nd = sch.emit("vector", lambda v, r=row[bi], o=orow[bi]: v.scalar_tensor_tensor(
                o[:], r[:], stat[:, 0:1], stat[:, 3:4].broadcast_to([128, E]),
                ALU.subtract, ALU.mult), inc="dv")
            if tt + 2 < NTT:
                sch.wait("sync", "dv", nd)
                ld[tt + 2] = sch.emit("sync", lambda sy, d=row[bi], tt=tt: sy.dma_start(
                    out=d[:], in_=io["outpre_dram"][(tt + 2) * 128:(tt + 3) * 128, :]),
                    inc=f"o6l{bi}", k=16)
            sch.wait("sync", "dv", nd)
            std[tt] = sch.emit("sync", lambda sy, o=orow[bi], tt=tt: sy.dma_start(
                out=io["out"][tt * 128:(tt + 1) * 128, :], in_=o[:]), inc=f"o6s{bi}", k=16)
        sch.wait("sync", "o6s0", sch.n("o6s0"))
        sch.wait("sync", "o6s1", sch.n("o6s1"))


_NC_CACHE = None


def _rope_tables():
    ROPE_BASE = 10000.0
    inv_freq = 1.0 / ROPE_BASE ** (np.arange(0, D, 2, dtype=np.float32) / D)
    pos = np.arange(S, dtype=np.float32)
    freqs = np.outer(pos, inv_freq)
    emb = np.concatenate([freqs, freqs], axis=-1)
    return np.cos(emb).astype(np.float32), np.sin(emb).astype(np.float32)


def kernel(L0_attn_x, L0_attn_W_qkv, L0_attn_W_out, L0_ffn_w_gate_up, L0_ffn_w_proj):
    global _NC_CACHE
    x = np.ascontiguousarray(np.asarray(L0_attn_x, dtype=np.float32))
    W_qkv = np.asarray(L0_attn_W_qkv, dtype=np.float32)
    W_out = np.asarray(L0_attn_W_out, dtype=np.float32)
    w_gu = np.asarray(L0_ffn_w_gate_up, dtype=np.float32)
    w_p = np.asarray(L0_ffn_w_proj, dtype=np.float32)

    wqkv_T = np.ascontiguousarray(W_qkv.T)
    wout_T = np.ascontiguousarray(W_out.T)
    perm = np.empty(HGU, dtype=np.int64)
    for j in range(NJJ):
        perm[2 * j * 128:(2 * j + 1) * 128] = np.arange(j * 128, (j + 1) * 128)
        perm[(2 * j + 1) * 128:(2 * j + 2) * 128] = HP + np.arange(j * 128, (j + 1) * 128)
    wgu_T = np.ascontiguousarray(w_gu[perm, :].T)
    wp_T = np.ascontiguousarray(w_p.T)

    cos, sin = _rope_tables()
    sinA = sin.copy()
    sinA[:, 0::2] = -sin[:, 0::2]

    trilm = np.zeros((128, 2 * CH), dtype=np.float32)
    for mh in range(2):
        for p in range(128):
            m = mh * 128 + p
            trilm[p, mh * CH + m:(mh + 1) * CH] = 1.0
    ident = np.eye(128, dtype=np.float32)
    ind2 = np.zeros((2, 128), dtype=np.float32)
    ind2[0, 0:64] = 1.0
    ind2[1, 64:128] = 1.0
    grp2 = np.ascontiguousarray(ind2.T)
    mgrp = np.zeros((128, 128), dtype=np.float32)
    mgrp[0:64, 0:64] = 1.0 / 128
    mgrp[64:128, 64:128] = 1.0 / 128
    onescol = np.ones((128, 1), dtype=np.float32)

    in_maps = []
    for c in range(8):
        b, half = c // 2, c % 2
        sl = slice(half * T, (half + 1) * T)
        xs = x[b, sl, :]
        m = {
            "x_T": np.ascontiguousarray(xs.T),
            "x_tok": np.ascontiguousarray(xs),
            "xaux_T": (np.ascontiguousarray(x[b, 0:T, :].T) if half == 1
                       else np.zeros((E, T), dtype=np.float32)),
            "wqkv_T": wqkv_T,
            "wout_T": wout_T,
            "wgu_T": wgu_T,
            "wp_T": wp_T,
            "cos_m": np.ascontiguousarray(cos[sl]),
            "sinA_m": np.ascontiguousarray(sinA[sl]),
            "cos_a": np.ascontiguousarray(cos[0:T]),
            "sinA_a": np.ascontiguousarray(sinA[0:T]),
            "auxones": np.full((T, 1), float(half), dtype=np.float32),
            "trilm": trilm,
            "ident": ident,
            "ind2": ind2,
            "grp2": grp2,
            "mgrp": mgrp,
            "onescol": onescol,
        }
        in_maps.append(m)

    if _NC_CACHE is None:
        _NC_CACHE = build_nc()
    res = run_bass_kernel_spmd(_NC_CACHE, in_maps, core_ids=list(range(8)))

    out = np.empty((B, S, E), dtype=np.float32)
    for c in range(8):
        b, half = c // 2, c % 2
        out[b, half * T:(half + 1) * T, :] = np.asarray(res.results[c]["out"], dtype=np.float32)
    return out


# version 25 (final)
# speedup vs baseline: 1.0065x; 1.0065x over previous
import os
import numpy as np
from contextlib import ExitStack

_PHASE_CAP = int(os.environ.get("KPHASES", "99"))

import concourse.bass as bass
import concourse.mybir as mybir
from concourse.bass_utils import run_bass_kernel_spmd

F32 = mybir.dt.float32
BF16 = mybir.dt.bfloat16
U32 = mybir.dt.uint32
AF = mybir.ActivationFunctionType
ALU = mybir.AluOpType
AX = mybir.AxisListType

B, S, E = 4, 2048, 2048
H, D = 16, 128
CH = 256
NCH = 4
T = 1024
NTT = 8
KC = 16
HGU = 16384
HP = 8192
NJJ = 64
LN_EPS = 1e-5
RMS_EPS = float(np.finfo(np.float32).eps)

ENGS = ("sync", "gpsimd", "tensor", "vector", "scalar")


class Sched:

    def __init__(self, nc, stack):
        self.nc = nc
        self.stack = stack
        self.q = {en: [] for en in ENGS}
        self.sems = {}
        for name in ("g16", "s16", "a16", "pe", "dv", "sc", "bar"):
            self._sem(name)

    def _sem(self, name):
        if name not in self.sems:
            self.sems[name] = [self.stack.enter_context(self.nc.semaphore(name)), 0]
        return self.sems[name]

    def n(self, sem):
        return self._sem(sem)[1]

    def emit(self, eng, fn, inc=None, k=1):
        if inc is None:
            self.q[eng].append(fn)
            return None
        h = self._sem(inc)[0]
        self._sem(inc)[1] += k

        def wrapped(e, fn=fn, h=h, k=k):
            fn(e).then_inc(h, k)

        self.q[eng].append(wrapped)
        return self.sems[inc][1]

    def inc(self, eng, sem, k=1):
        h = self._sem(sem)[0]
        self._sem(sem)[1] += k
        self.q[eng].append(lambda e, h=h, k=k: e.sem_inc(h, k))
        return self.sems[sem][1]

    def wait(self, eng, sem, val):
        if val is None or val <= 0:
            return
        h = self._sem(sem)[0]
        self.q[eng].append(lambda e, h=h, v=val: e.wait_ge(h, v))

    def barrier(self):
        h = self.sems["bar"][0]
        self.sems["bar"][1] += 5
        tgt = self.sems["bar"][1]
        for en in ENGS:
            def f(e, h=h, tgt=tgt):
                e.sem_inc(h, 1)
                e.wait_ge(h, tgt)
            self.q[en].append(f)

    def flush(self, block):
        for en in ENGS:
            fns = self.q[en]
            if not fns:
                continue

            def run(e, fns=fns):
                for fn in fns:
                    fn(e)

            getattr(block, en)(run)
        self.q = {en: [] for en in ENGS}


def build_nc():
    nc = bass.Bass("TRN2", target_bir_lowering=False)

    def din(name, shape, dtype=F32):
        return nc.dram_tensor(name, list(shape), dtype, kind="ExternalInput").ap()

    io = {}
    io["x_T"] = din("x_T", [E, T])
    io["x_tok"] = din("x_tok", [T, E])
    io["xaux_T"] = din("xaux_T", [E, T])
    io["wqkv_T"] = din("wqkv_T", [E, 3 * E])
    io["wout_T"] = din("wout_T", [E, E])
    io["wgu_T"] = din("wgu_T", [E, HGU])
    io["wp_T"] = din("wp_T", [HP, E])
    io["cos_m"] = din("cos_m", [T, D])
    io["sinA_m"] = din("sinA_m", [T, D])
    io["cos_a"] = din("cos_a", [T, D])
    io["sinA_a"] = din("sinA_a", [T, D])
    io["auxones"] = din("auxones", [T, 1])
    io["trilm"] = din("trilm", [128, 2 * CH])
    io["ident"] = din("ident", [128, 128])
    io["ind2"] = din("ind2", [2, 128])
    io["grp2"] = din("grp2", [128, 2])
    io["mgrp"] = din("mgrp", [128, 128])
    io["onescol"] = din("onescol", [128, 1])
    io["out"] = nc.dram_tensor("out", [T, E], F32, kind="ExternalOutput").ap()

    io["wqgu_dram"] = nc.dram_tensor("wqgu_scr", [NJJ, KC, 128, 256], BF16, kind="Internal").ap()
    io["wqp_dram"] = nc.dram_tensor("wqp_scr", [64, 128, E], BF16, kind="Internal").ap()
    io["y1_dram"] = nc.dram_tensor("y1_scr", [T, E], F32, kind="Internal").ap()
    io["hsil_dram"] = nc.dram_tensor("hsil_scr", [64, 128, T], BF16, kind="Internal").ap()
    io["outpre_dram"] = nc.dram_tensor("outpre_scr", [T, E], F32, kind="Internal").ap()
    io["inv_dram"] = nc.dram_tensor("inv_scr", [1, T], F32, kind="Internal").ap()
    if os.environ.get("KDBG") in ("wq", "wq2"):
        io["dbg_wqgu"] = nc.dram_tensor("dbg_wqgu", [NJJ, KC, 128, 256], BF16, kind="ExternalOutput").ap()
        io["dbg_wqp"] = nc.dram_tensor("dbg_wqp", [64, 128, E], BF16, kind="ExternalOutput").ap()
    if os.environ.get("KDBG") == "qkv":
        io["dbg_qT"] = nc.dram_tensor("dbg_qT", [128, H, T], BF16, kind="ExternalOutput").ap()
        io["dbg_kT"] = nc.dram_tensor("dbg_kT", [128, H, T], BF16, kind="ExternalOutput").ap()
        io["dbg_v"] = nc.dram_tensor("dbg_v", [128, NTT, H, 129], BF16, kind="ExternalOutput").ap()
        io["dbg_sinit"] = nc.dram_tensor("dbg_sinit", [128, H, 129], F32, kind="ExternalOutput").ap()
    if os.environ.get("KDBG") == "numt":
        io["dbg_numT"] = nc.dram_tensor("dbg_numT", [128, H, T], BF16, kind="ExternalOutput").ap()
    if os.environ.get("KDBG") == "y1":
        io["dbg_y1"] = nc.dram_tensor("dbg_y1", [T, E], F32, kind="ExternalOutput").ap()
    if os.environ.get("KDBG") == "p4":
        io["dbg_hsil"] = nc.dram_tensor("dbg_hsil", [64, 128, T], BF16, kind="ExternalOutput").ap()
        io["dbg_inv"] = nc.dram_tensor("dbg_inv", [128, NTT], F32, kind="ExternalOutput").ap()
    if os.environ.get("KDBG") == "outpre":
        io["dbg_outpre"] = nc.dram_tensor("dbg_outpre", [T, E], F32, kind="ExternalOutput").ap()

    st = ExitStack()
    with st:
        sch = Sched(nc, st)
        block = st.enter_context(nc.Block())

        sb = {}

        def alloc(stack, name, shape, dtype):
            t = stack.enter_context(nc.sbuf_tensor(name, list(shape), dtype))
            sb[name] = t
            return t

        ident_bf = alloc(st, "ident_bf", [128, 128], BF16)
        mgrp_bf = alloc(st, "mgrp_bf", [128, 128], BF16)
        onescol_bf = alloc(st, "onescol_bf", [128, 1], BF16)
        inv_sb = alloc(st, "inv_sb", [128, NTT], F32)
        two_col = alloc(st, "two_col", [128, 1], F32)
        sch.emit("vector", lambda v: v.memset(two_col[:], 2.0))

        for dname, iname in (("ident_bf", "ident"),
                             ("mgrp_bf", "mgrp"), ("onescol_bf", "onescol")):
            sch.emit("gpsimd",
                     lambda g, d=sb[dname], s=io[iname]: g.dma_start(out=d[:], in_=s),
                     inc="g16", k=16)
        sch.wait("gpsimd", "g16", sch.n("g16"))
        sch.barrier()

        st_q = ExitStack()
        quant_alloc(nc, st_q, sb)
        st_num = ExitStack()
        st_attn = ExitStack()
        st_y1 = ExitStack()
        alloc(st_attn, "qT", [128, H, T], BF16)
        alloc(st_attn, "kT", [128, H, T], BF16)
        alloc(st_attn, "v_sb", [128, NTT, H, 129], BF16)
        alloc(st_attn, "S_init", [128, H, 129], F32)

        qgen = quant_units(nc, sch, io, sb)
        qstate = {"pulled": 0, "units": 0.0}
        NQT = (256 + 128) * 3
        W1A, W1B, W2, W3, W4 = 1.0, 1.0, 0.6, 1.0, 1.45
        NUNITS_W = 96 * W1A + 64 * W1B + 64 * W2 + 8 * W3 + 128 * W4

        def tick(w=1.0):
            qstate["units"] += w
            want = min(NQT, int(round(qstate["units"] * (NQT / NUNITS_W))))
            while qstate["pulled"] < want:
                next(qgen)
                qstate["pulled"] += 1

        def drain(upto):
            while qstate["pulled"] < upto:
                next(qgen)
                qstate["pulled"] += 1

        phase_p1a(nc, sch, io, sb, lambda: tick(W1A))
        sch.barrier()
        phase_p1b(nc, sch, io, sb, lambda: tick(W1B))
        sch.barrier()
        sb["numT"] = st_num.enter_context(
            nc.sbuf_tensor("numT", [128, H, T], BF16, side="right"))
        sb["trilm_bf"] = st_num.enter_context(
            nc.sbuf_tensor("trilm_bf", [128, 2 * CH], BF16, side="right"))
        sch.emit("gpsimd", lambda g: g.dma_start(out=sb["trilm_bf"][:], in_=io["trilm"]),
                 inc="ld", k=16)
        sch.wait("vector", "ld", sch.n("ld"))
        phase_p2(nc, sch, io, sb, lambda: tick(W2))
        sch.barrier()
        st_attn.close()
        sb["y1T"] = st_y1.enter_context(nc.sbuf_tensor("y1T", [128, KC, T], BF16))
        phase_p3(nc, sch, io, sb, lambda: tick(W3))
        sch.barrier()
        st_num.close()
        phase_p4(nc, sch, io, sb, lambda: tick(W4), drain)
        drain(NQT)
        next(qgen)
        if "dbg_wqgu" in io:
            sch.emit("sync", lambda sy: sy.dma_start(out=io["dbg_wqgu"], in_=io["wqgu_dram"]),
                     inc="s16", k=16)
            sch.emit("sync", lambda sy: sy.dma_start(out=io["dbg_wqp"], in_=io["wqp_dram"]),
                     inc="s16", k=16)
            sch.wait("sync", "s16", sch.n("s16"))
        if "dbg_y1" in io:
            sch.emit("sync", lambda sy: sy.dma_start(out=io["dbg_y1"], in_=io["y1_dram"]),
                     inc="s16", k=16)
            sch.wait("sync", "s16", sch.n("s16"))
        if "dbg_outpre" in io:
            pass
        if "dbg_hsil" in io:
            sch.emit("sync", lambda sy: sy.dma_start(out=io["dbg_hsil"], in_=io["hsil_dram"]),
                     inc="s16", k=16)
            sch.emit("sync", lambda sy: sy.dma_start(out=io["dbg_inv"], in_=io["inv_dram"]
                     .rearrange("o (tt p) -> (o p) tt", p=128), allow_slow_non_contiguous=True),
                     inc="s16", k=16)
            sch.wait("sync", "s16", sch.n("s16"))
        sch.barrier()
        if "dbg_wqp" in io and os.environ.get("KDBG") == "wq2":
            sch.emit("sync", lambda sy: sy.dma_start(out=io["dbg_wqp"], in_=io["wqp_dram"]),
                     inc="s16", k=16)
            sch.wait("sync", "s16", sch.n("s16"))
        st_y1.close()
        st_q.close()
        phase_p5(nc, sch, io, sb)
        sch.barrier()
        if "dbg_outpre" in io:
            sch.emit("sync", lambda sy: sy.dma_start(out=io["dbg_outpre"], in_=io["outpre_dram"]),
                     inc="s16", k=16)
            sch.wait("sync", "s16", sch.n("s16"))
        phase_p6(nc, sch, io, sb)
        sch.flush(block)

    return nc


def quant_alloc(nc, stack, sb):
    FREE = 1024
    sb["q_wbf"] = [stack.enter_context(nc.sbuf_tensor(f"q_wbf{i}", [128, FREE], BF16)) for i in range(3)]
    sb["q_a"] = [stack.enter_context(nc.sbuf_tensor(f"q_a{i}", [128, FREE], BF16)) for i in range(2)]
    sb["q_t"] = [stack.enter_context(nc.sbuf_tensor(f"q_t{i}", [128, FREE], BF16)) for i in range(2)]
    sb["q_sbc"] = stack.enter_context(nc.sbuf_tensor("q_sbc", [128, FREE], BF16))
    sb["q_sg"] = stack.enter_context(nc.sbuf_tensor("q_sg", [128, FREE], BF16))
    sb["q_pb"] = [stack.enter_context(nc.psum_tensor(f"q_pb{i}", [128, 512], F32)) for i in range(2)]


def quant_units(nc, sch, io, sb):
    mgrp_bf = sb["mgrp_bf"]
    w_bf, a_sb = sb["q_wbf"], sb["q_a"]
    t_sb, sbc2, sg_sb = sb["q_t"], sb["q_sbc"], sb["q_sg"]
    pb = sb["q_pb"]
    FREE = 1024
    NQ = FREE // 512

    iters = []
    for hsl in range(16):
        for kc in range(KC):
            src = io["wgu_T"][kc * 128:(kc + 1) * 128, hsl * 1024:(hsl + 1) * 1024]
            dst = io["wqgu_dram"][hsl * 4:(hsl + 1) * 4, kc].rearrange("j p c -> p j c")
            iters.append((src, dst, True))
    for ch in range(64):
        for half in range(2):
            src = io["wp_T"][ch * 128:(ch + 1) * 128, half * 1024:(half + 1) * 1024]
            dst = io["wqp_dram"][ch][:, half * 1024:(half + 1) * 1024]
            iters.append((src, dst, False))
    NIT = len(iters)

    in_dma = [None] * (NIT + 4)
    a_done = [None] * (NIT + 4)
    smm = [[None] * NQ for _ in range(NIT)]
    eva = [[None] * NQ for _ in range(NIT)]
    chain_done = [None] * NIT
    out_dma = [None] * NIT

    def emit_in_dma(it):
        if it >= NIT:
            return
        src = iters[it][0]
        if it >= 3:
            sch.wait("gpsimd", "dv", chain_done[it - 3])
        in_dma[it] = sch.emit(
            "gpsimd", lambda g, d=w_bf[it % 3], s=src: g.dma_start(out=d[:], in_=s),
            inc=f"qg{it % 3}", k=16)

    emit_in_dma(0)
    emit_in_dma(1)

    two_col = sb["two_col"]
    for it, (src, dst, is_gu) in enumerate(iters):
        bi = it % 2
        bw = it % 3
        sch.wait("scalar", f"qg{bw}", in_dma[it])
        if it >= 2:
            sch.wait("scalar", "pe", smm[it - 2][NQ - 1])
            sch.wait("scalar", "dv", chain_done[it - 2])
        a_done[it] = sch.emit("scalar", lambda sc, d=a_sb[bi], s=w_bf[bw]: sc.activation(
            out=d[:], in_=s[:], func=AF.Abs, scale=two_col[:]), inc="sc")
        yield ("s1", it)

        sch.wait("tensor", "sc", a_done[it])
        for qt in range(NQ):
            if it >= 1 or qt >= 1:
                prev = eva[it][qt - 1] if qt >= 1 else eva[it - 1][NQ - 1]
                sch.wait("tensor", "dv", prev)
            smm[it][qt] = sch.emit(
                "tensor",
                lambda t, ps=pb[qt % 2], sl=a_sb[bi][:, qt * 512:(qt + 1) * 512]:
                    t.matmul(ps[:], mgrp_bf[:], sl, start=True, stop=True),
                inc="pe")
        yield ("s2", it)

        for qt in range(NQ):
            sch.wait("gpsimd", "pe", smm[it][qt])
            rly = sch.inc("gpsimd", "qe")
            sch.wait("vector", "qe", rly)
            eva[it][qt] = sch.emit(
                "vector",
                lambda v, d=sbc2[:, qt * 512:(qt + 1) * 512], ps=pb[qt % 2]:
                    v.tensor_single_scalar(d, ps[:], 1e-8, ALU.max),
                inc="dv")
        sch.emit("vector", lambda v: v.tensor_single_scalar(
            sg_sb[:], sbc2[:], 2.0 ** -8, ALU.mult))
        if it >= 2:
            sch.wait("vector", f"qs{bi}", out_dma[it - 2])
        sch.emit("vector", lambda v, a=a_sb[bi], t2=t_sb[bi]: v.tensor_tensor(
            t2[:], a[:], sbc2[:], ALU.subtract))
        sch.emit("vector", lambda v, a=a_sb[bi], t2=t_sb[bi]: v.tensor_tensor(
            a[:], t2[:], sg_sb[:], ALU.is_gt))
        sch.emit("vector", lambda v, d=sg_sb, s=w_bf[bw]: v.tensor_single_scalar(
            d[:].bitcast(U32), s[:].bitcast(U32), 0x80008000, ALU.bitwise_and))
        sch.emit("vector", lambda v, a=a_sb[bi], t2=t_sb[bi]: v.tensor_tensor(
            t2[:], a[:], sbc2[:], ALU.mult))
        sch.emit("vector", lambda v, t2=t_sb[bi]: v.tensor_tensor(
            t2[:].bitcast(U32), t2[:].bitcast(U32), sg_sb[:].bitcast(U32), ALU.bitwise_or))
        chain_done[it] = sch.emit("vector", lambda v: v.engine_nop(), inc="dv")
        emit_in_dma(it + 2)
        sch.wait("sync", "dv", chain_done[it])
        oblk = dst.shape[1] if dst.ndim == 3 else 1
        out_dma[it] = sch.emit(
            "sync", lambda sy, d=dst, ss=t_sb[bi], oblk=oblk: sy.dma_start(
                out=d,
                in_=(ss[:].rearrange("p (b c) -> p b c", b=oblk) if oblk > 1 else ss[:])),
            inc=f"qs{bi}", k=16)
        if is_gu and (it + 1) % KC == 0:
            sch.wait("sync", "qs0", sch.n("qs0"))
            sch.wait("sync", "qs1", sch.n("qs1"))
            sch.inc("sync", "qk")
        yield ("s3", it)

    sch.wait("sync", "qs0", sch.n("qs0"))
    sch.wait("sync", "qs1", sch.n("qs1"))
    while True:
        yield ("done", NIT)


def rope_elu(sch, ps, dst_bf, cos_t, sinA_t, tt, r1, r2, do_rope=True):
    if do_rope:
        cosv = cos_t[:, tt, :].unsqueeze(1).broadcast_to([128, 4, 128])
        sinv = sinA_t[:, tt, :].unsqueeze(1).broadcast_to([128, 4, 128])
        psv = ps[:].rearrange("p (h d) -> p h d", d=128)
        r1v = r1[:].rearrange("p (h d) -> p h d", d=128)
        r2v = r2[:].rearrange("p (h d) -> p h d", d=128)
        sch.emit("vector", lambda v: v.tensor_tensor(r1v, psv, cosv, ALU.mult))
        sch.emit("vector", lambda v: v.tensor_tensor(
            r2v[:, :, bass.ds(0, 64, 2)], psv[:, :, bass.ds(1, 64, 2)],
            sinv[:, :, bass.ds(0, 64, 2)], ALU.mult))
        sch.emit("vector", lambda v: v.tensor_tensor(
            r2v[:, :, bass.ds(1, 64, 2)], psv[:, :, bass.ds(0, 64, 2)],
            sinv[:, :, bass.ds(1, 64, 2)], ALU.mult))
        sch.emit("vector", lambda v: v.tensor_tensor(r1[:], r1[:], r2[:], ALU.add))
        src = r1
    else:
        sch.emit("vector", lambda v: v.tensor_copy(r1[:], ps[:]))
        src = r1
    sch.emit("vector", lambda v, s=src: v.tensor_single_scalar(r2[:], s[:], 0.0, ALU.min))
    mid = sch.emit("vector", lambda v: v.engine_nop(), inc="dv")
    sch.wait("scalar", "dv", mid)
    edone = sch.emit("scalar", lambda s: s.activation(out=r2[:], in_=r2[:], func=AF.Exp),
                     inc="sc")
    sch.wait("vector", "sc", edone)
    sch.emit("vector", lambda v, s=src, d=dst_bf: v.scalar_tensor_tensor(
        d[:], s[:], 0.0, r2[:], ALU.max, ALU.add))
    return sch.emit("vector", lambda v: v.engine_nop(), inc="dv")


def phase_p1a(nc, sch, io, sb, tick):
    with ExitStack() as ph:
        xT_bf = ph.enter_context(nc.sbuf_tensor("xT_bf", [128, KC, T], BF16))
        wslab = [ph.enter_context(nc.sbuf_tensor(f"wslab{i}", [128, KC, 512], BF16)) for i in range(2)]
        cos_t = ph.enter_context(nc.sbuf_tensor("cos_t", [128, NTT, D], F32))
        sinA_t = ph.enter_context(nc.sbuf_tensor("sinA_t", [128, NTT, D], F32))
        r1 = ph.enter_context(nc.sbuf_tensor("rope1", [128, 512], F32))
        r2 = ph.enter_context(nc.sbuf_tensor("rope2", [128, 512], F32))
        sc_bf = [ph.enter_context(nc.sbuf_tensor(f"scr_bf{i}", [128, 512], BF16)) for i in range(2)]
        ps_mm = [ph.enter_context(nc.psum_tensor(f"p1_mm{i}", [128, 512], F32)) for i in range(2)]
        ps_tr = [ph.enter_context(nc.psum_tensor(f"p1_tr{i}", [128, 128], F32)) for i in range(2)]

        qT, kT, v_sb = sb["qT"], sb["kT"], sb["v_sb"]
        ident_bf = sb["ident_bf"]

        sch.emit("gpsimd", lambda g, d=xT_bf, s=io["x_T"]: g.dma_start(
            out=d[:], in_=s.rearrange("(kc p) t -> p kc t", p=128)), inc="ld", k=16)
        sch.emit("sync", lambda sy, d=cos_t, s=io["cos_m"]: sy.dma_start(
            out=d[:], in_=s.rearrange("(tt p) d -> p tt d", p=128)), inc="ld", k=16)
        sch.emit("sync", lambda sy, d=sinA_t, s=io["sinA_m"]: sy.dma_start(
            out=d[:], in_=s.rearrange("(tt p) d -> p tt d", p=128)), inc="ld", k=16)
        ld0 = sch.n("ld")
        sch.wait("vector", "ld", ld0)
        ones_set = sch.emit("vector", lambda v, d=v_sb: v.memset(d[:, :, :, 128:129], 1.0),
                            inc="dv")

        wdma = [None] * 12
        tile_mm = {}
        tile_ev = {}
        tr_guard = {0: None, 1: None}
        pending = None

        def flush_pending():
            nonlocal pending
            if pending is None:
                return
            idx, buf, dstT, fs, tt, rope_done = pending
            last_ev = None
            for j in range(4):
                h = (fs % 4) * 4 + j
                tp = j % 2
                sch.wait("tensor", "dv", rope_done)
                sch.wait("tensor", "dv", tr_guard[tp])
                trm = sch.emit("tensor", lambda t, ps=ps_tr[tp], sbuf=sc_bf[buf], j=j:
                               t.matmul(ps[:], sbuf[:, j * 128:(j + 1) * 128], ident_bf[:],
                                        start=True, stop=True), inc="pe")
                sch.wait("vector", "pe", trm)
                last_ev = sch.emit("vector", lambda v, ps=ps_tr[tp], d=dstT, h=h, tt=tt:
                                   v.tensor_copy(d[:, h, tt * 128:(tt + 1) * 128], ps[:]),
                                   inc="dv")
                tr_guard[tp] = last_ev
            tile_ev[idx] = last_ev
            pending = None

        for fs in range(12):
            bi = fs % 2
            if fs >= 2:
                sch.wait("gpsimd", "pe", tile_mm[(fs - 2) * NTT + NTT - 1])
            wdma[fs] = sch.emit("gpsimd", lambda g, d=wslab[bi], s=io["wqkv_T"], fs=fs:
                                g.dma_start(out=d[:], in_=s[:, fs * 512:(fs + 1) * 512]
                                            .rearrange("(kc p) f -> p kc f", p=128)),
                                inc=f"ws{bi}", k=16)
            kind = "q" if fs < 4 else ("k" if fs < 8 else "v")
            for tt in range(NTT):
                idx = fs * NTT + tt
                pp = idx % 2
                sch.wait("tensor", f"ws{bi}", wdma[fs])
                if fs == 0 and tt == 0:
                    sch.wait("tensor", "ld", ld0)
                if idx - 2 >= 0:
                    sch.wait("tensor", "dv", tile_ev.get(idx - 2))
                for kc in range(KC):
                    m = sch.emit("tensor", lambda t, ps=ps_mm[pp], x=xT_bf, w=wslab[bi],
                                 kc=kc, tt=tt: t.matmul(
                                     ps[:], x[:, kc, tt * 128:(tt + 1) * 128], w[:, kc, :],
                                     start=(kc == 0), stop=(kc == KC - 1)),
                                 inc="pe" if kc == KC - 1 else None)
                    if kc == KC - 1:
                        tile_mm[idx] = m
                flush_pending()
                sch.wait("vector", "pe", tile_mm[idx])
                if kind == "v":
                    sch.wait("vector", "dv", ones_set)
                    for j in range(4):
                        h = (fs - 8) * 4 + j
                        sch.emit("vector", lambda v, ps=ps_mm[pp], tt=tt, h=h, j=j, d=v_sb:
                                 v.tensor_copy(d[:, tt, h, 0:128], ps[:, j * 128:(j + 1) * 128]))
                    tile_ev[idx] = sch.emit("vector", lambda v: v.engine_nop(), inc="dv")
                else:
                    buf = idx % 2
                    rope_done = _rope_into_flat(sch, ps_mm[pp], sc_bf[buf][:], cos_t, sinA_t,
                                                tt, r1, r2)
                    tile_ev[idx] = rope_done
                    pending = (idx, buf, qT if kind == "q" else kT, fs, tt, rope_done)
                tick()
        flush_pending()


def _rope_into_flat(sch, ps, dst_ap, cos_t, sinA_t, tt, r1, r2):
    cosv = cos_t[:, tt, :].unsqueeze(1).broadcast_to([128, 4, 128])
    sinv = sinA_t[:, tt, :].unsqueeze(1).broadcast_to([128, 4, 128])
    psv = ps[:].rearrange("p (h d) -> p h d", d=128)
    r1v = r1[:].rearrange("p (h d) -> p h d", d=128)
    r2v = r2[:].rearrange("p (h d) -> p h d", d=128)
    sch.emit("vector", lambda v: v.tensor_tensor(r1v, psv, cosv, ALU.mult))
    sch.emit("vector", lambda v: v.tensor_tensor(
        r2v[:, :, bass.ds(0, 64, 2)], psv[:, :, bass.ds(1, 64, 2)],
        sinv[:, :, bass.ds(0, 64, 2)], ALU.mult))
    sch.emit("vector", lambda v: v.tensor_tensor(
        r2v[:, :, bass.ds(1, 64, 2)], psv[:, :, bass.ds(0, 64, 2)],
        sinv[:, :, bass.ds(1, 64, 2)], ALU.mult))
    sch.emit("vector", lambda v: v.tensor_tensor(r1[:], r1[:], r2[:], ALU.add))
    sch.emit("vector", lambda v: v.tensor_single_scalar(r2[:], r1[:], 0.0, ALU.min))
    mid = sch.emit("vector", lambda v: v.engine_nop(), inc="dv")
    sch.wait("scalar", "dv", mid)
    edone = sch.emit("scalar", lambda s: s.activation(out=r2[:], in_=r2[:], func=AF.Exp),
                     inc="sc")
    sch.wait("vector", "sc", edone)
    sch.emit("vector", lambda v, d=dst_ap: v.scalar_tensor_tensor(
        d, r1[:], 0.0, r2[:], ALU.max, ALU.add))
    return sch.emit("vector", lambda v: v.engine_nop(), inc="dv")


def phase_p1b(nc, sch, io, sb, tick):
    with ExitStack() as ph:
        xaT_bf = ph.enter_context(nc.sbuf_tensor("xaT_bf", [128, KC, T], BF16))
        wslab = [ph.enter_context(nc.sbuf_tensor("waslab0", [128, KC, 512], BF16))] * 2
        cos_t = ph.enter_context(nc.sbuf_tensor("cos_ta", [128, NTT, D], F32))
        sinA_t = ph.enter_context(nc.sbuf_tensor("sinA_ta", [128, NTT, D], F32))
        auxo = ph.enter_context(nc.sbuf_tensor("auxo_bf", [128, NTT], BF16))
        r1 = ph.enter_context(nc.sbuf_tensor("arope1", [128, 512], F32))
        r2 = ph.enter_context(nc.sbuf_tensor("arope2", [128, 512], F32))
        kaux = ph.enter_context(nc.sbuf_tensor("kaux", [128, NTT, 512], BF16))
        vaug = ph.enter_context(nc.sbuf_tensor("vaug", [128, 4, 129], BF16))
        ps_mm = [ph.enter_context(nc.psum_tensor(f"pa_mm{i}", [128, 512], F32)) for i in range(2)]
        ps_sa = [ph.enter_context(nc.psum_tensor(f"pa_sa{i}", [128, 129], F32)) for i in range(4)]

        S_init = sb["S_init"]

        sch.emit("gpsimd", lambda g, d=xaT_bf, s=io["xaux_T"]: g.dma_start(
            out=d[:], in_=s.rearrange("(kc p) t -> p kc t", p=128)), inc="ld", k=16)
        sch.emit("gpsimd", lambda g, d=auxo, s=io["auxones"]: g.dma_start(
            out=d[:], in_=s.rearrange("(tt p) c -> p (tt c)", p=128),
            allow_slow_non_contiguous=True), inc="ld", k=16)
        sch.emit("sync", lambda sy, d=cos_t, s=io["cos_a"]: sy.dma_start(
            out=d[:], in_=s.rearrange("(tt p) d -> p tt d", p=128)), inc="ld", k=16)
        sch.emit("sync", lambda sy, d=sinA_t, s=io["sinA_a"]: sy.dma_start(
            out=d[:], in_=s.rearrange("(tt p) d -> p tt d", p=128)), inc="ld", k=16)
        lds = sch.n("ld")
        sch.wait("vector", "ld", lds)

        prev_kmm_last = None
        prev_vmm_last = None
        prev_smm_last = None
        for i in range(4):
            if prev_vmm_last is not None:
                sch.wait("gpsimd", "pe", prev_vmm_last)
            wk = sch.emit("gpsimd", lambda g, d=wslab[0], s=io["wqkv_T"], i=i: g.dma_start(
                out=d[:], in_=s[:, (4 + i) * 512:(5 + i) * 512].rearrange("(kc p) f -> p kc f", p=128)),
                inc="ws0", k=16)
            kev = [None] * NTT
            kmm = [None] * NTT
            for tt in range(NTT):
                pp = tt % 2
                sch.wait("tensor", "ws0", wk)
                if i == 0 and tt == 0:
                    sch.wait("tensor", "ld", lds)
                if tt >= 2:
                    sch.wait("tensor", "dv", kev[tt - 2])
                for kc in range(KC):
                    m = sch.emit("tensor", lambda t, ps=ps_mm[pp], x=xaT_bf, w=wslab[0],
                                 kc=kc, tt=tt: t.matmul(
                                     ps[:], x[:, kc, tt * 128:(tt + 1) * 128], w[:, kc, :],
                                     start=(kc == 0), stop=(kc == KC - 1)),
                                 inc="pe" if kc == KC - 1 else None)
                    if kc == KC - 1:
                        kmm[tt] = m
                sch.wait("vector", "pe", kmm[tt])
                if prev_smm_last is not None and tt == 0:
                    sch.wait("vector", "pe", prev_smm_last)
                kev[tt] = _rope_into_flat(sch, ps_mm[pp], kaux[:, tt, :], cos_t, sinA_t,
                                          tt, r1, r2)
                tick()
            prev_kmm_last = kmm[NTT - 1]
            sch.wait("gpsimd", "pe", prev_kmm_last)
            wv = sch.emit("gpsimd", lambda g, d=wslab[1], s=io["wqkv_T"], i=i: g.dma_start(
                out=d[:], in_=s[:, (8 + i) * 512:(9 + i) * 512].rearrange("(kc p) f -> p kc f", p=128)),
                inc="ws0", k=16)
            smm = [None] * NTT
            vev = [None] * NTT
            for tt in range(NTT):
                pp = tt % 2
                sch.wait("tensor", "ws0", wv)
                sch.wait("tensor", "dv", kev[tt])
                if tt >= 2:
                    sch.wait("tensor", "dv", vev[tt - 2])
                vmm = None
                for kc in range(KC):
                    m = sch.emit("tensor", lambda t, ps=ps_mm[pp], x=xaT_bf, w=wslab[1],
                                 kc=kc, tt=tt: t.matmul(
                                     ps[:], x[:, kc, tt * 128:(tt + 1) * 128], w[:, kc, :],
                                     start=(kc == 0), stop=(kc == KC - 1)),
                                 inc="pe" if kc == KC - 1 else None)
                    if kc == KC - 1:
                        vmm = m
                prev_vmm_last = vmm
                sch.wait("vector", "pe", vmm)
                if tt >= 1:
                    sch.wait("vector", "pe", smm[tt - 1])
                for j in range(4):
                    sch.emit("vector", lambda v, ps=ps_mm[pp], j=j, d=vaug:
                             v.tensor_copy(d[:, j, 0:128], ps[:, j * 128:(j + 1) * 128]))
                sch.emit("vector", lambda v, d=vaug, s=auxo, tt=tt: v.tensor_copy(
                    d[:, :, 128:129], s[:, tt:tt + 1].unsqueeze(1).broadcast_to([128, 4, 1])))
                vev[tt] = sch.emit("vector", lambda v: v.engine_nop(), inc="dv")
                sch.wait("tensor", "dv", vev[tt])
                for j in range(4):
                    m = sch.emit("tensor", lambda t, ps=ps_sa[j], ka=kaux, va=vaug, j=j, tt=tt:
                                 t.matmul(ps[:], ka[:, tt, j * 128:(j + 1) * 128], va[:, j, :],
                                          start=(tt == 0), stop=(tt == NTT - 1),
                                          skip_group_check=True),
                                 inc="pe" if j == 3 else None)
                    if j == 3:
                        smm[tt] = m
                tick()
            sch.wait("vector", "pe", smm[NTT - 1])
            for j in range(4):
                sch.emit("vector", lambda v, ps=ps_sa[j], h=4 * i + j, d=S_init:
                         v.tensor_copy(d[:, h, :], ps[:]))
            sed = sch.emit("vector", lambda v: v.engine_nop(), inc="dv")
            sch.wait("tensor", "dv", sed)
            prev_smm_last = smm[NTT - 1]


def phase_p2(nc, sch, io, sb, tick):
    NU = H * NCH
    with ExitStack() as ph:
        attT = [ph.enter_context(nc.sbuf_tensor(f"attT{i}", [128, 2 * CH], BF16)) for i in range(2)]
        ktok2 = [ph.enter_context(nc.sbuf_tensor(f"ktok{i}", [128, 2, 128], BF16)) for i in range(2)]
        nsc = [ph.enter_context(nc.sbuf_tensor(f"nsc{i}", [128, 256], BF16)) for i in range(2)]
        denb = [ph.enter_context(nc.sbuf_tensor(f"denb{i}", [128, 2], F32)) for i in range(2)]
        Sf = ph.enter_context(nc.sbuf_tensor("Sf", [128, H, 129], F32))
        Sbf = ph.enter_context(nc.sbuf_tensor("Sbf", [128, H, 129], BF16))
        ps_att = [ph.enter_context(nc.psum_tensor(f"p2_at{i}", [128, 512], F32)) for i in range(2)]
        ps_num = [ph.enter_context(nc.psum_tensor(f"p2_nm{i}", [128, 512], F32)) for i in range(2)]
        ps_S = [ph.enter_context(nc.psum_tensor(f"p2_S{i}", [128, 512], F32)) for i in range(2)]

        qT, kT, v_sb, S_init, numT = sb["qT"], sb["kT"], sb["v_sb"], sb["S_init"], sb["numT"]
        trilm_bf, ident_bf = sb["trilm_bf"], sb["ident_bf"]

        sch.emit("vector", lambda v: v.tensor_copy(Sf[:], S_init[:]))
        sbf0 = sch.emit("vector", lambda v: v.tensor_copy(Sbf[:], Sf[:]), inc="dv")

        ktr_last = [None] * NU
        amm_last = [None] * NU
        nmm_last = [None] * NU
        summ_last = [None] * NU
        trm_last = [None] * NU
        ktok_ok = [None] * NU
        att_ok = [None] * NU
        nok = [None] * NU
        numTev_ok = [None] * NU
        Sevac_ok = [None] * NU
        sbf_mark = [sbf0] * H

        def emit_ktr(u):
            if u >= NU:
                return
            c, hh = divmod(u, H)
            B = u % 2
            if u >= 2:
                sch.wait("tensor", "dv", Sevac_ok[u - 2])
            m = None
            for mh in range(2):
                m = sch.emit("tensor", lambda t, B=B, mh=mh, c=c, hh=hh: t.matmul(
                    ps_S[B][:, 256 + mh * 128:384 + mh * 128],
                    kT[:, hh, c * 256 + mh * 128:c * 256 + mh * 128 + 128],
                    ident_bf[:], start=True, stop=True, skip_group_check=True),
                    inc="pe" if mh == 1 else None)
            ktr_last[u] = m

        def emit_amm(u):
            if u >= NU:
                return
            c, hh = divmod(u, H)
            B = u % 2
            if u >= 2:
                sch.wait("tensor", "dv", numTev_ok[u - 2])
            m = None
            for mh in range(2):
                m = sch.emit("tensor", lambda t, B=B, mh=mh, c=c, hh=hh: t.matmul(
                    ps_att[B][:, mh * 256:(mh + 1) * 256],
                    kT[:, hh, c * 256 + mh * 128:c * 256 + mh * 128 + 128],
                    qT[:, hh, c * 256:(c + 1) * 256],
                    start=True, stop=True, skip_group_check=True),
                    inc="pe" if mh == 1 else None)
            amm_last[u] = m

        def emit_ktokev(u):
            if u >= NU:
                return
            B = u % 2
            sch.wait("vector", "pe", ktr_last[u])
            if u >= 2:
                sch.wait("vector", "pe", summ_last[u - 2])
            for mh in range(2):
                sch.emit("vector", lambda v, B=B, mh=mh: v.tensor_copy(
                    ktok2[B][:, mh, :], ps_S[B][:, 256 + mh * 128:384 + mh * 128]))
            ktok_ok[u] = sch.emit("vector", lambda v: v.engine_nop(), inc="dv")

        def emit_attev(u):
            if u >= NU:
                return
            B = u % 2
            sch.wait("vector", "pe", amm_last[u])
            if u >= 2:
                sch.wait("vector", "pe", nmm_last[u - 2])
            sch.emit("vector", lambda v, B=B: v.tensor_tensor(
                attT[B][:], ps_att[B][:], trilm_bf[:], ALU.mult))
            att_ok[u] = sch.emit("vector", lambda v: v.engine_nop(), inc="dv")

        emit_ktr(0)
        emit_amm(0)
        emit_ktokev(0)
        emit_attev(0)

        for u in range(NU):
            c, hh = divmod(u, H)
            B = u % 2
            emit_ktr(u + 1)
            emit_amm(u + 1)
            emit_ktokev(u + 1)
            emit_attev(u + 1)
            sch.wait("tensor", "dv", att_ok[u])
            sch.wait("tensor", "dv", sbf_mark[hh])
            first = True
            m = None
            for lh in range(2):
                for mh in range(2):
                    m = sch.emit("tensor", lambda t, B=B, lh=lh, mh=mh, c=c, hh=hh, st=first:
                                 t.matmul(ps_num[B][:, lh * 256:lh * 256 + 129],
                                          attT[B][:, mh * 256 + lh * 128:mh * 256 + lh * 128 + 128],
                                          v_sb[:, c * 2 + mh, hh, :],
                                          start=st, stop=False, skip_group_check=True))
                    first = False
                m = sch.emit("tensor", lambda t, B=B, lh=lh, c=c, hh=hh: t.matmul(
                    ps_num[B][:, lh * 256:lh * 256 + 129],
                    qT[:, hh, c * 256 + lh * 128:c * 256 + lh * 128 + 128],
                    Sbf[:, hh, :], start=False, stop=(lh == 1), skip_group_check=True),
                    inc="pe" if lh == 1 else None)
            nmm_last[u] = m
            sch.wait("tensor", "dv", ktok_ok[u])
            if u >= 2:
                sch.wait("tensor", "dv", Sevac_ok[u - 2])
            m = None
            for mh in range(2):
                m = sch.emit("tensor", lambda t, B=B, mh=mh, c=c, hh=hh: t.matmul(
                    ps_S[B][:, 0:129], ktok2[B][:, mh, :], v_sb[:, c * 2 + mh, hh, :],
                    start=(mh == 0), stop=(mh == 1), skip_group_check=True),
                    inc="pe" if mh == 1 else None)
            summ_last[u] = m
            sch.wait("vector", "pe", nmm_last[u])
            f1 = sch.emit("vector", lambda v, B=B: v.tensor_single_scalar(
                denb[B][:], ps_num[B][:, bass.ds(128, 2, 256)], 1e-6, ALU.max), inc="dv")
            sch.wait("vector", "dv", f1)
            f2 = sch.emit("vector", lambda v, B=B: v.reciprocal(denb[B][:], denb[B][:]),
                          inc="dv")
            sch.wait("vector", "dv", f2)
            for lh in range(2):
                sch.emit("vector", lambda v, B=B, lh=lh: v.tensor_tensor(
                    nsc[B][:, lh * 128:(lh + 1) * 128], ps_num[B][:, lh * 256:lh * 256 + 128],
                    denb[B][:, lh:lh + 1].broadcast_to([128, 128]), ALU.mult))
            nok[u] = sch.emit("vector", lambda v: v.engine_nop(), inc="dv")
            sch.wait("vector", "pe", summ_last[u])
            sch.emit("vector", lambda v, B=B, hh=hh: v.tensor_tensor(
                Sf[:, hh, :], Sf[:, hh, :], ps_S[B][:, 0:129], ALU.add))
            Sevac_ok[u] = sch.emit("vector", lambda v, hh=hh: v.tensor_copy(
                Sbf[:, hh, :], Sf[:, hh, :]), inc="dv")
            sbf_mark[hh] = Sevac_ok[u]
            sch.wait("tensor", "dv", nok[u])
            m = None
            for lh in range(2):
                m = sch.emit("tensor", lambda t, B=B, lh=lh: t.matmul(
                    ps_att[B][:, lh * 128:(lh + 1) * 128], nsc[B][:, lh * 128:(lh + 1) * 128],
                    ident_bf[:], start=True, stop=True, skip_group_check=True),
                    inc="pe" if lh == 1 else None)
            trm_last[u] = m
            sch.wait("vector", "pe", trm_last[u])
            for lh in range(2):
                mk = sch.emit("vector", lambda v, B=B, lh=lh, c=c, hh=hh: v.tensor_copy(
                    numT[:, hh, c * 256 + lh * 128:c * 256 + lh * 128 + 128],
                    ps_att[B][:, lh * 128:(lh + 1) * 128]), inc="dv" if lh == 1 else None)
            numTev_ok[u] = mk
            tick()


def phase_p3(nc, sch, io, sb, tick):
    with ExitStack() as ph:
        wo_bf = ph.enter_context(nc.sbuf_tensor("wo_bf", [128, KC, E], BF16))
        xtok = [ph.enter_context(nc.sbuf_tensor(f"xtok{i}", [128, E], F32)) for i in range(2)]
        y1f = ph.enter_context(nc.sbuf_tensor("y1f", [128, E], F32))
        sq = ph.enter_context(nc.sbuf_tensor("sq3", [128, E], F32))
        stat = ph.enter_context(nc.sbuf_tensor("stat3", [128, 4], F32))
        y1bf = [ph.enter_context(nc.sbuf_tensor(f"y1bf{i}", [128, E], BF16)) for i in range(2)]
        ps_mm = [ph.enter_context(nc.psum_tensor(f"p3_mm{i}", [128, 512], F32)) for i in range(4)]
        ps_tr = [ph.enter_context(nc.psum_tensor(f"p3_tr{i}", [128, 128], F32)) for i in range(2)]

        numT, y1T, ident_bf = sb["numT"], sb["y1T"], sb["ident_bf"]

        wo = sch.emit("gpsimd", lambda g, d=wo_bf, s=io["wout_T"]: g.dma_start(
            out=d[:], in_=s.rearrange("(kc p) f -> p kc f", p=128)), inc="ld", k=16)

        xd = [None] * NTT
        for tt in range(2):
            xd[tt] = sch.emit("sync", lambda sy, d=xtok[tt % 2], s=io["x_tok"], tt=tt:
                              sy.dma_start(out=d[:], in_=s[tt * 128:(tt + 1) * 128, :]),
                              inc=f"xk{tt % 2}", k=16)

        tr_guard = {0: None, 1: None}
        res_done = [None] * NTT
        prev_y1_store = None
        last_trm = None
        pending = None

        def flush_pending():
            nonlocal pending, last_trm
            if pending is None:
                return
            ptt, buf, y1ok = pending
            for kc in range(KC):
                tp = kc % 2
                sch.wait("tensor", "dv", y1ok)
                sch.wait("tensor", "dv", tr_guard[tp])
                trm = sch.emit("tensor", lambda t, ps=ps_tr[tp], ybf=y1bf[buf], kc=kc: t.matmul(
                    ps[:], ybf[:, kc * 128:(kc + 1) * 128], ident_bf[:],
                    start=True, stop=True), inc="pe")
                last_trm = trm
                sch.wait("vector", "pe", trm)
                tr_guard[tp] = sch.emit(
                    "vector", lambda v, ps=ps_tr[tp], kc=kc, ptt=ptt: v.tensor_copy(
                        y1T[:, kc, ptt * 128:(ptt + 1) * 128], ps[:]), inc="dv")
            pending = None

        for tt in range(NTT):
            sch.wait("tensor", "ld", wo)
            if tt >= 1:
                sch.wait("tensor", "dv", res_done[tt - 1])
            pmm = None
            for h in range(H):
                for os in range(4):
                    pmm = sch.emit("tensor", lambda t, h=h, os=os, tt=tt: t.matmul(
                        ps_mm[os][:], numT[:, h, tt * 128:(tt + 1) * 128],
                        wo_bf[:, h, os * 512:(os + 1) * 512],
                        start=(h == 0), stop=(h == H - 1), skip_group_check=True),
                        inc="pe" if (h == H - 1 and os == 3) else None)
            flush_pending()
            sch.wait("vector", "pe", pmm)
            sch.wait("vector", f"xk{tt % 2}", xd[tt])
            if prev_y1_store is not None:
                sch.wait("vector", "y1s", prev_y1_store)
            for os in range(4):
                sch.emit("vector", lambda v, os=os, tt=tt: v.tensor_tensor(
                    y1f[:, os * 512:(os + 1) * 512], ps_mm[os][:],
                    xtok[tt % 2][:, os * 512:(os + 1) * 512], ALU.add))
            res_done[tt] = sch.emit("vector", lambda v: v.engine_nop(), inc="dv")
            if tt + 2 < NTT:
                sch.wait("sync", "dv", res_done[tt])
                xd[tt + 2] = sch.emit("sync", lambda sy, d=xtok[tt % 2], s=io["x_tok"], tt=tt:
                                      sy.dma_start(out=d[:], in_=s[(tt + 2) * 128:(tt + 3) * 128, :]),
                                      inc=f"xk{tt % 2}", k=16)
            sch.emit("vector", lambda v: v.tensor_tensor(sq[:], y1f[:], y1f[:], ALU.mult))
            sch.emit("vector", lambda v: v.reduce_sum(stat[:, 0:1], y1f[:], AX.X))
            sch.emit("vector", lambda v: v.reduce_sum(stat[:, 1:2], sq[:], AX.X))
            fm = sch.emit("vector", lambda v: v.tensor_single_scalar(stat[:, 0:1], stat[:, 0:1],
                                                                     1.0 / E, ALU.mult), inc="dv")
            sch.wait("vector", "dv", fm)
            sch.emit("vector", lambda v: v.tensor_single_scalar(stat[:, 1:2], stat[:, 1:2],
                                                                1.0 / E, ALU.mult))
            sch.emit("vector", lambda v: v.tensor_tensor(stat[:, 2:3], stat[:, 0:1],
                                                         stat[:, 0:1], ALU.mult))
            sch.emit("vector", lambda v: v.tensor_tensor(stat[:, 1:2], stat[:, 1:2],
                                                         stat[:, 2:3], ALU.subtract))
            vst = sch.emit("vector", lambda v: v.tensor_single_scalar(
                stat[:, 1:2], stat[:, 1:2], LN_EPS, ALU.add), inc="dv")
            sch.wait("vector", "dv", vst)
            sch.wait("scalar", "dv", vst)
            sqd = sch.emit("scalar", lambda s: s.activation(out=stat[:, 3:4], in_=stat[:, 1:2],
                                                            func=AF.Sqrt), inc="sc")
            sch.wait("vector", "sc", sqd)
            fr = sch.emit("vector", lambda v: v.reciprocal(stat[:, 3:4], stat[:, 3:4]), inc="dv")
            sch.wait("vector", "dv", fr)
            sch.emit("vector", lambda v: v.tensor_tensor(
                y1f[:], y1f[:], stat[:, 0:1].broadcast_to([128, E]), ALU.subtract))
            sch.emit("vector", lambda v: v.tensor_tensor(
                y1f[:], y1f[:], stat[:, 3:4].broadcast_to([128, E]), ALU.mult))
            if last_trm is not None:
                sch.wait("vector", "pe", last_trm)
            sch.emit("vector", lambda v, d=y1bf[tt % 2]: v.tensor_copy(d[:], y1f[:]))
            y1ok = sch.emit("vector", lambda v: v.engine_nop(), inc="dv")
            sch.wait("sync", "dv", y1ok)
            prev_y1_store = sch.emit("sync", lambda sy, tt=tt: sy.dma_start(
                out=io["y1_dram"][tt * 128:(tt + 1) * 128, :], in_=y1f[:]), inc="y1s", k=16)
            pending = (tt, tt % 2, y1ok)
            tick()
        flush_pending()
        sch.wait("sync", "y1s", sch.n("y1s"))
        sch.wait("sync", "xk0", sch.n("xk0"))
        sch.wait("sync", "xk1", sch.n("xk1"))


def phase_p4(nc, sch, io, sb, tick, drain):
    NWB = 4
    with ExitStack() as ph:
        wgu = [ph.enter_context(nc.sbuf_tensor(f"wgu{i}", [128, KC, 256], BF16)) for i in range(NWB)]
        silu_f = ph.enter_context(nc.sbuf_tensor("silu_f", [128, 512], F32))
        hsil = [ph.enter_context(nc.sbuf_tensor(f"hsil{i}", [128, T], BF16)) for i in range(2)]
        sqb = [ph.enter_context(nc.sbuf_tensor(f"sqb{i}", [128, 512], BF16)) for i in range(2)]
        ssqrow = ph.enter_context(nc.sbuf_tensor("ssqrow", [1, T], F32))
        invrow = ph.enter_context(nc.sbuf_tensor("invrow", [1, T], F32))
        ps_g = [ph.enter_context(nc.psum_tensor(f"p4_g{i}", [128, 512], F32)) for i in range(2)]
        ps_u = [ph.enter_context(nc.psum_tensor(f"p4_u{i}", [128, 512], F32)) for i in range(2)]
        ps_q = [ph.enter_context(nc.psum_tensor(f"p4_q{i}", [128, 512], F32)) for i in range(2)]

        y1T, onescol_bf, inv_sb = sb["y1T"], sb["onescol_bf"], sb["inv_sb"]

        wdma = [None] * NJJ
        umm_last = [None] * NJJ
        hs_dma = [None] * NJJ
        hdone_list = []
        ssq_done = [None] * (2 * NJJ)
        sq_pend = None
        prev_hdone = None

        def emit_ssq():
            nonlocal sq_pend
            if sq_pend is None:
                return
            psq, pjj, pth, pgi = sq_pend
            sch.wait("tensor", "sc", psq)
            ssq_done[pgi] = sch.emit("tensor", lambda t, ps=ps_q[pth], sq=sqb[pgi % 2],
                                     pjj=pjj: t.matmul(
                ps[0:1, :], onescol_bf[:], sq[:], start=(pjj == 0), stop=(pjj == NJJ - 1),
                skip_group_check=True), inc="pe")
            sq_pend = None

        def emit_wdma(jj):
            if jj >= NJJ:
                return
            ws = jj % NWB
            drain((jj // 4 + 1) * KC * 3)
            sch.wait("scalar", "qk", jj // 4 + 1)
            if jj >= NWB:
                sch.wait("scalar", "pe", umm_last[jj - NWB])
            wdma[jj] = sch.emit("scalar", lambda g, d=wgu[ws], s=io["wqgu_dram"], jj=jj:
                                g.dma_start(out=d[:], in_=s[jj].rearrange("kc p c -> p kc c")),
                                inc=f"wg{ws}", k=16)

        emit_wdma(0)
        emit_wdma(1)
        for jj in range(NJJ):
            wslot = jj % NWB
            hslot = jj % 2
            emit_wdma(jj + 2)
            if jj >= 2:
                sch.wait("vector", f"hs{hslot}", hs_dma[jj - 2])
            for th in range(2):
                gu_idx = jj * 2 + th
                pp = gu_idx % 2
                sch.wait("tensor", f"wg{wslot}", wdma[jj])
                if gu_idx >= 2:
                    sch.wait("tensor", "dv", hdone_list[gu_idx - 2])
                umm = None
                for kc in range(KC):
                    sch.emit("tensor", lambda t, ps=ps_g[pp], w=wgu[wslot], kc=kc, th=th:
                             t.matmul(ps[:], w[:, kc, 0:128], y1T[:, kc, th * 512:(th + 1) * 512],
                                      start=(kc == 0), stop=(kc == KC - 1), skip_group_check=True))
                for kc in range(KC):
                    m = sch.emit("tensor", lambda t, ps=ps_u[pp], w=wgu[wslot], kc=kc, th=th:
                                 t.matmul(ps[:], w[:, kc, 128:256], y1T[:, kc, th * 512:(th + 1) * 512],
                                          start=(kc == 0), stop=(kc == KC - 1), skip_group_check=True),
                                 inc="pe" if kc == KC - 1 else None)
                    if kc == KC - 1:
                        umm = m
                if th == 1:
                    umm_last[jj] = umm
                emit_ssq()
                sch.wait("scalar", "pe", umm)
                if prev_hdone is not None:
                    sch.wait("scalar", "dv", prev_hdone)
                sdone = sch.emit("scalar", lambda s, ps=ps_g[pp]: s.activation(
                    out=silu_f[:], in_=ps[:], func=AF.Silu), inc="sc")
                sch.wait("vector", "sc", sdone)
                hdone = sch.emit("vector", lambda v, ps=ps_u[pp], d=hsil[hslot], th=th:
                                 v.tensor_tensor(d[:, th * 512:(th + 1) * 512], silu_f[:],
                                                 ps[:], ALU.mult), inc="dv")
                prev_hdone = hdone
                hdone_list.append(hdone)
                sch.wait("scalar", "dv", hdone)
                if gu_idx >= 2:
                    sch.wait("scalar", "pe", ssq_done[gu_idx - 2])
                sq_done = sch.emit("scalar", lambda s, d=sqb[gu_idx % 2], src=hsil[hslot],
                                   th=th: s.activation(
                    out=d[:], in_=src[:, th * 512:(th + 1) * 512], func=AF.Square), inc="sc")
                sq_pend = (sq_done, jj, th, gu_idx)
                tick()
            sch.wait("sync", "dv", hdone_list[jj * 2 + 1])
            hs_dma[jj] = sch.emit("sync", lambda sy, s=hsil[hslot], jj=jj: sy.dma_start(
                out=io["hsil_dram"][jj], in_=s[:]), inc=f"hs{hslot}", k=16)
        emit_ssq()

        sch.wait("vector", "pe", ssq_done[2 * NJJ - 1])
        sch.emit("vector", lambda v: v.tensor_copy(ssqrow[:, 0:512], ps_q[0][0:1, :]))
        sch.emit("vector", lambda v: v.tensor_copy(ssqrow[:, 512:1024], ps_q[1][0:1, :]))
        sch.emit("vector", lambda v: v.tensor_scalar(ssqrow[:], ssqrow[:], 1.0 / HP, RMS_EPS,
                                                     ALU.mult, ALU.add))
        vr = sch.emit("vector", lambda v: v.engine_nop(), inc="dv")
        sch.wait("scalar", "dv", vr)
        sqd = sch.emit("scalar", lambda s: s.activation(out=invrow[:], in_=ssqrow[:],
                                                        func=AF.Sqrt), inc="sc")
        sch.wait("vector", "sc", sqd)
        rd = sch.emit("vector", lambda v: v.reciprocal(invrow[:], invrow[:]), inc="dv")
        sch.wait("sync", "dv", rd)
        st1 = sch.emit("sync", lambda sy: sy.dma_start(
            out=io["inv_dram"], in_=invrow[:]), inc="inv", k=16)
        sch.wait("sync", "inv", st1)
        sch.emit("sync", lambda sy: sy.dma_start(
            out=inv_sb[:], in_=io["inv_dram"].rearrange("o (tt p) -> (o p) tt", p=128),
            allow_slow_non_contiguous=True), inc="inv", k=16)
        sch.wait("sync", "inv", sch.n("inv"))
        sch.wait("sync", "hs0", sch.n("hs0"))
        sch.wait("sync", "hs1", sch.n("hs1"))


def phase_p5(nc, sch, io, sb):
    with ExitStack() as ph:
        wqp = ph.enter_context(nc.sbuf_tensor("wqp_sb", [128, 64, 512], BF16))
        hs = [ph.enter_context(nc.sbuf_tensor(f"p5h{i}", [128, 4 * T], BF16)) for i in range(2)]
        y1dg = ph.enter_context(nc.sbuf_tensor("y1dg", [128, NTT, 512], F32))
        op = ph.enter_context(nc.sbuf_tensor("op5", [128, NTT, 512], F32))
        ps_o = [ph.enter_context(nc.psum_tensor(f"p5_o{i}", [128, 512], F32)) for i in range(8)]

        inv_sb = sb["inv_sb"]

        prev_ev = None
        prev_store = None
        k_mm = {}
        for dg in range(4):
            if dg > 0:
                sch.wait("gpsimd", "pe", k_mm[(dg - 1, 15)])
            wd = None
            for part in range(4):
                wd = sch.emit("gpsimd", lambda g, d=wqp, s=io["wqp_dram"], dg=dg, part=part:
                              g.dma_start(out=d[:, part * 16:(part + 1) * 16, :],
                                          in_=s[part * 16:(part + 1) * 16, :, dg * 512:(dg + 1) * 512]
                                          .rearrange("k p c -> p k c")),
                              inc="p5w", k=16)
            if prev_ev is not None:
                sch.wait("sync", "dv", prev_ev)
            yd = sch.emit("sync", lambda sy, d=y1dg, s=io["y1_dram"], dg=dg: sy.dma_start(
                out=d[:], in_=s[:, dg * 512:(dg + 1) * 512].rearrange("(tt p) c -> p tt c", p=128)),
                inc="p5y", k=16)
            for kb in range(16):
                hb = kb % 2
                if (dg, kb - 2) in k_mm:
                    sch.wait("scalar", "pe", k_mm[(dg, kb - 2)])
                elif dg > 0 and kb < 2:
                    sch.wait("scalar", "pe", k_mm[(dg - 1, 14 + kb)])
                hd = sch.emit("scalar", lambda s, d=hs[hb], kb=kb: s.dma_start(
                    out=d[:].rearrange("p (c t) -> p c t", c=4),
                    in_=io["hsil_dram"][kb * 4:(kb + 1) * 4].rearrange("c p t -> p c t")),
                    inc=f"ha{hb}", k=16)
                sch.wait("tensor", f"ha{hb}", hd)
                sch.wait("tensor", "p5w", wd)
                if dg > 0 and kb == 0:
                    sch.wait("tensor", "dv", prev_ev)
                for kk in range(4):
                    k = kb * 4 + kk
                    for tt in range(NTT):
                        k_mm[(dg, kb)] = sch.emit("tensor", lambda t, k=k, kk=kk, tt=tt, hb=hb:
                            t.matmul(ps_o[tt][:], hs[hb][:, kk * T + tt * 128:kk * T + (tt + 1) * 128],
                                     wqp[:, k, :],
                                     start=(k == 0), stop=(k == 63), skip_group_check=True),
                            inc="pe" if tt == NTT - 1 else None)
            sch.wait("vector", "pe", k_mm[(dg, 15)])
            sch.wait("vector", "p5y", yd)
            if prev_store is not None:
                sch.wait("vector", "p5s", prev_store)
            for tt in range(NTT):
                sch.emit("vector", lambda v, tt=tt: v.scalar_tensor_tensor(
                    op[:, tt, :], ps_o[tt][:], inv_sb[:, tt:tt + 1], y1dg[:, tt, :],
                    ALU.mult, ALU.add))
            prev_ev = sch.emit("vector", lambda v: v.engine_nop(), inc="dv")
            sch.wait("sync", "dv", prev_ev)
            prev_store = sch.emit("sync", lambda sy, dg=dg: sy.dma_start(
                out=io["outpre_dram"][:, dg * 512:(dg + 1) * 512]
                    .rearrange("(tt p) c -> p tt c", p=128), in_=op[:]), inc="p5s", k=16)
        sch.wait("sync", "p5s", sch.n("p5s"))


def phase_p6(nc, sch, io, sb):
    with ExitStack() as ph:
        row = [ph.enter_context(nc.sbuf_tensor(f"p6r{i}", [128, E], F32)) for i in range(2)]
        sq = [ph.enter_context(nc.sbuf_tensor(f"sq6{i}", [128, E], F32)) for i in range(2)]
        stat = ph.enter_context(nc.sbuf_tensor("stat6", [128, 4], F32))
        orow = [ph.enter_context(nc.sbuf_tensor(f"p6o{i}", [128, E], F32)) for i in range(2)]

        ld = [None] * NTT
        std = [None] * NTT
        sq_used = [None] * NTT
        for tt in range(2):
            ld[tt] = sch.emit("sync", lambda sy, d=row[tt % 2], tt=tt: sy.dma_start(
                out=d[:], in_=io["outpre_dram"][tt * 128:(tt + 1) * 128, :]), inc=f"o6l{tt % 2}", k=16)
        for tt in range(NTT):
            bi = tt % 2
            sch.wait("vector", f"o6l{bi}", ld[tt])
            if tt >= 2:
                sch.wait("vector", f"o6s{bi}", std[tt - 2])
            sch.wait("scalar", f"o6l{bi}", ld[tt])
            if tt >= 2:
                sch.wait("scalar", "dv", sq_used[tt - 2])
            sqd0 = sch.emit("scalar", lambda sc, r=row[bi], q=sq[bi]: sc.activation(
                out=q[:], in_=r[:], func=AF.Square), inc="sc")
            sch.emit("vector", lambda v, r=row[bi]: v.reduce_sum(stat[:, 0:1], r[:], AX.X))
            sch.wait("vector", "sc", sqd0)
            sq_used[tt] = sch.emit("vector", lambda v, q=sq[bi]: v.reduce_sum(
                stat[:, 1:2], q[:], AX.X), inc="dv")
            fm = sch.emit("vector", lambda v: v.tensor_single_scalar(stat[:, 0:1], stat[:, 0:1],
                                                                     1.0 / E, ALU.mult), inc="dv")
            sch.wait("vector", "dv", fm)
            sch.emit("vector", lambda v: v.tensor_single_scalar(stat[:, 1:2], stat[:, 1:2],
                                                                1.0 / E, ALU.mult))
            sch.emit("vector", lambda v: v.tensor_tensor(stat[:, 2:3], stat[:, 0:1],
                                                         stat[:, 0:1], ALU.mult))
            sch.emit("vector", lambda v: v.tensor_tensor(stat[:, 1:2], stat[:, 1:2],
                                                         stat[:, 2:3], ALU.subtract))
            vst = sch.emit("vector", lambda v: v.tensor_single_scalar(
                stat[:, 1:2], stat[:, 1:2], LN_EPS, ALU.add), inc="dv")
            sch.wait("vector", "dv", vst)
            sch.wait("scalar", "dv", vst)
            sqd = sch.emit("scalar", lambda s: s.activation(out=stat[:, 3:4], in_=stat[:, 1:2],
                                                            func=AF.Sqrt), inc="sc")
            sch.wait("vector", "sc", sqd)
            fr = sch.emit("vector", lambda v: v.reciprocal(stat[:, 3:4], stat[:, 3:4]), inc="dv")
            sch.wait("vector", "dv", fr)
            nd = sch.emit("vector", lambda v, r=row[bi], o=orow[bi]: v.scalar_tensor_tensor(
                o[:], r[:], stat[:, 0:1], stat[:, 3:4].broadcast_to([128, E]),
                ALU.subtract, ALU.mult), inc="dv")
            if tt + 2 < NTT:
                sch.wait("sync", "dv", nd)
                ld[tt + 2] = sch.emit("sync", lambda sy, d=row[bi], tt=tt: sy.dma_start(
                    out=d[:], in_=io["outpre_dram"][(tt + 2) * 128:(tt + 3) * 128, :]),
                    inc=f"o6l{bi}", k=16)
            sch.wait("sync", "dv", nd)
            std[tt] = sch.emit("sync", lambda sy, o=orow[bi], tt=tt: sy.dma_start(
                out=io["out"][tt * 128:(tt + 1) * 128, :], in_=o[:]), inc=f"o6s{bi}", k=16)
        sch.wait("sync", "o6s0", sch.n("o6s0"))
        sch.wait("sync", "o6s1", sch.n("o6s1"))


_NC_CACHE = None


def _rope_tables():
    ROPE_BASE = 10000.0
    inv_freq = 1.0 / ROPE_BASE ** (np.arange(0, D, 2, dtype=np.float32) / D)
    pos = np.arange(S, dtype=np.float32)
    freqs = np.outer(pos, inv_freq)
    emb = np.concatenate([freqs, freqs], axis=-1)
    return np.cos(emb).astype(np.float32), np.sin(emb).astype(np.float32)


def kernel(L0_attn_x, L0_attn_W_qkv, L0_attn_W_out, L0_ffn_w_gate_up, L0_ffn_w_proj):
    global _NC_CACHE
    x = np.ascontiguousarray(np.asarray(L0_attn_x, dtype=np.float32))
    W_qkv = np.asarray(L0_attn_W_qkv, dtype=np.float32)
    W_out = np.asarray(L0_attn_W_out, dtype=np.float32)
    w_gu = np.asarray(L0_ffn_w_gate_up, dtype=np.float32)
    w_p = np.asarray(L0_ffn_w_proj, dtype=np.float32)

    wqkv_T = np.ascontiguousarray(W_qkv.T)
    wout_T = np.ascontiguousarray(W_out.T)
    perm = np.empty(HGU, dtype=np.int64)
    for j in range(NJJ):
        perm[2 * j * 128:(2 * j + 1) * 128] = np.arange(j * 128, (j + 1) * 128)
        perm[(2 * j + 1) * 128:(2 * j + 2) * 128] = HP + np.arange(j * 128, (j + 1) * 128)
    wgu_T = np.ascontiguousarray(w_gu[perm, :].T)
    wp_T = np.ascontiguousarray(w_p.T)

    cos, sin = _rope_tables()
    sinA = sin.copy()
    sinA[:, 0::2] = -sin[:, 0::2]

    trilm = np.zeros((128, 2 * CH), dtype=np.float32)
    for mh in range(2):
        for p in range(128):
            m = mh * 128 + p
            trilm[p, mh * CH + m:(mh + 1) * CH] = 1.0
    ident = np.eye(128, dtype=np.float32)
    ind2 = np.zeros((2, 128), dtype=np.float32)
    ind2[0, 0:64] = 1.0
    ind2[1, 64:128] = 1.0
    grp2 = np.ascontiguousarray(ind2.T)
    mgrp = np.zeros((128, 128), dtype=np.float32)
    mgrp[0:64, 0:64] = 1.0 / 128
    mgrp[64:128, 64:128] = 1.0 / 128
    onescol = np.ones((128, 1), dtype=np.float32)

    in_maps = []
    for c in range(8):
        b, half = c // 2, c % 2
        sl = slice(half * T, (half + 1) * T)
        xs = x[b, sl, :]
        m = {
            "x_T": np.ascontiguousarray(xs.T),
            "x_tok": np.ascontiguousarray(xs),
            "xaux_T": (np.ascontiguousarray(x[b, 0:T, :].T) if half == 1
                       else np.zeros((E, T), dtype=np.float32)),
            "wqkv_T": wqkv_T,
            "wout_T": wout_T,
            "wgu_T": wgu_T,
            "wp_T": wp_T,
            "cos_m": np.ascontiguousarray(cos[sl]),
            "sinA_m": np.ascontiguousarray(sinA[sl]),
            "cos_a": np.ascontiguousarray(cos[0:T]),
            "sinA_a": np.ascontiguousarray(sinA[0:T]),
            "auxones": np.full((T, 1), float(half), dtype=np.float32),
            "trilm": trilm,
            "ident": ident,
            "ind2": ind2,
            "grp2": grp2,
            "mgrp": mgrp,
            "onescol": onescol,
        }
        in_maps.append(m)

    if _NC_CACHE is None:
        _NC_CACHE = build_nc()
    res = run_bass_kernel_spmd(_NC_CACHE, in_maps, core_ids=list(range(8)))

    out = np.empty((B, S, E), dtype=np.float32)
    for c in range(8):
        b, half = c // 2, c % 2
        out[b, half * T:(half + 1) * T, :] = np.asarray(res.results[c]["out"], dtype=np.float32)
    return out
